# version 25 (final)
# baseline (speedup 1.0000x reference)
import contextlib
import numpy as np
import concourse.bass as bass
import concourse.mybir as mybir
from concourse.bass_utils import run_bass_kernel_spmd
from concourse.bass_core_rust import IndirectOffsetOnAxis
from concourse.aot_env import allow_ici_all_to_all
from concourse import library_config

F32 = mybir.dt.float32
U32 = mybir.dt.uint32
BF16 = mybir.dt.bfloat16
AF = mybir.ActivationFunctionType
ALU = mybir.AluOpType
AX = mybir.AxisListType

T, D, DI, NST, DC, E, H = 16384, 1024, 2048, 16, 4, 8, 4096
CAP, CAPP = 4301, 4352
NCORE = 8
TCH = 1024
EPS = 1e-5
RG = [list(range(NCORE))]

def _derive():
    g = globals()
    g["SHARD"] = CAPP + 2 * (T // NCORE)
    g["NCH"] = T // TCH
    g["DSL"] = DI // NCORE
    g["TOK"] = T // NCORE
    g["NTT"] = (T // NCORE) // 128
    g["NF"] = TCH // 512
    g["RB"] = CAPP // 128
    g["TCOL"] = (CAPP + 2 * (T // NCORE)) // 128
    g["NGRP"] = (T // NCORE) // 512
_derive()


class Sched:
    def __init__(self):
        self.items = []
        self.ctr = {}
        self.sems = {}

    def reg(self, name, handle):
        self.sems[name] = handle
        self.ctr[name] = 0

    def op(self, eng, make, sem=None, k=1, waits=()):
        waits = list(waits)
        if eng == "vector" and sem == "vC":
            waits.append(("vC", self.ctr["vC"]))
        elif eng == "scalar" and sem == "sC":
            waits.append(("sC", self.ctr["sC"]))
        if sem is not None:
            self.ctr[sem] += k
        tgt = self.ctr[sem] if sem is not None else None
        waits = tuple((n, v) for n, v in waits if v > 0)

        def fn(e, make=make, sem=sem, k=k, waits=waits):
            for wname, wval in waits:
                e.wait_ge(self.sems[wname], wval)
            ins = make(e)
            if ins is not None and sem is not None:
                ins.then_inc(self.sems[sem], k)
        self.items.append((eng, fn))
        return tgt

    def now(self, *names):
        return [(n, self.ctr[n]) for n in names]

    def all_now(self):
        return [(n, v) for n, v in self.ctr.items() if v > 0]


def build():
    nc = bass.Bass("TRN2", target_bir_lowering=False, detect_race_conditions=False)

    def din(name, shape, dt=F32):
        return nc.dram_tensor(name, shape, dt, kind="ExternalInput").ap()

    xT = din("xT", [D, T])
    winT = din("winT", [D, 2 * DSL])
    convw = din("convw", [128, 2, DC])
    convb = din("convb", [128, 2])
    xpT = din("xpT", [128, 2, 33])
    dtw = din("dtw", [128, 2])
    dtb = din("dtb", [128, 2])
    alog = din("alog", [128, 2, NST])
    dpar = din("dpar", [128, 2])
    woutT = din("woutT", [DI, D])
    wg = din("wg", [D, E])
    w1s = din("w1s", [H // 128, 128, 8 * 128])
    w2s = din("w2s", [E, 128, H])
    b1r = din("b1r", [128, H // 128])
    b2r = din("b2r", [128, E])
    xrows = din("xrows", [TOK, D])
    ident = din("ident", [128, 128])
    lexcl = din("lexcl", [128, 128])
    lex16 = din("lex16", [NTT, NTT * 128])
    selrep = din("selrep", [E, 128])
    ones16 = din("ones16", [NTT, 1])
    ones128 = din("ones128", [128, 1])
    iota8 = din("iota8", [128, E])
    dump1 = din("dump1", [128, NTT])
    dump2 = din("dump2", [128, NTT])
    tokv = din("tokv", [128, NTT])

    out = nc.dram_tensor("out", [TOK, D], F32, kind="ExternalOutput").ap()

    ssm_in = [nc.dram_tensor(f"ssm_in{j}", [33, TCH], F32).ap() for j in range(NCH)]
    ssm_out = [nc.dram_tensor(f"ssm_out{j}", [33, TCH], F32).ap() for j in range(NCH)]
    yg_pair = [nc.dram_tensor(f"yg_pair{h}", [NCORE, TOK // 512, 128, 512], F32).ap()
               for h in range(DSL // 128)]
    yg_me = [nc.dram_tensor(f"yg_me{h}", [NCORE, TOK // 512, 128, 512], F32).ap()
             for h in range(DSL // 128)]
    cnt_in = nc.dram_tensor("cnt_in", [1, E], F32).ap()
    cnt_out = nc.dram_tensor("cnt_out", [NCORE, E], F32).ap()
    tbl_in = nc.dram_tensor("tbl_in", [E, SHARD], F32).ap()
    tbl_out = nc.dram_tensor("tbl_out", [E, SHARD], F32).ap()
    xf_in = nc.dram_tensor("xf_in", [TOK, D], F32).ap()
    xf_out = nc.dram_tensor("xf_out", [T, D], F32).ap()
    y_in = nc.dram_tensor("y_in", [CAPP, D], F32).ap()
    y_out = nc.dram_tensor("y_out", [E * CAPP, D], F32).ap()
    stg_d = nc.dram_tensor("stg_d", [NTT, E], F32).ap()
    w1bf = nc.dram_tensor("w1bf", [H // 128, 128, 2 * 8 * 128], BF16).ap()
    w2bf = nc.dram_tensor("w2bf", [E, 128, 2 * H], BF16).ap()

    SEMS = ("tC vC sC gC "
            "sW sXT0 sXT1 sDTR sB0 sB1 sC0 sC1 sWO sXR sTOT sCNT sTB sXF4 sXF5 sXF6 sXF7 sXFIN sOUT "
            "sW10 sW11 sW20 sW21 sWS "
            "gSSM gYG gYG2 gYGK0 gYGK1 gYGK2 gYGK3 gZERO gSCAT gCNT gROW gYIN gR40 gR41 gR42 gR43 "
            "ccAR ccAG ccXF ccCNT ccTBL ccY").split()

    with contextlib.ExitStack() as stk:
        def sb(name, shape, dt=F32):
            return stk.enter_context(nc.sbuf_tensor(name, shape, dt))

        c_xpT = sb("c_xpT", [128, 2, 33])
        c_dtw = sb("c_dtw", [128, 2]); c_dtb = sb("c_dtb", [128, 2])
        c_A = sb("c_A", [128, 2, NST]); c_D = sb("c_D", [128, 2])
        c_conv = sb("c_conv", [128, 2, DC]); c_cb = sb("c_cb", [128, 2])
        c_wg = sb("c_wg", [128, 8, E])
        c_id = sb("c_id", [128, 128]); c_lex = sb("c_lex", [128, 128])
        c_selrep = sb("c_selrep", [E, 128])
        c_ones16 = sb("c_ones16", [NTT, 1]); c_ones128 = sb("c_ones128", [128, 1])
        c_iota8 = sb("c_iota8", [128, E])
        c_dump1 = sb("c_dump1", [128, NTT]); c_dump2 = sb("c_dump2", [128, NTT])
        c_tokv = sb("c_tokv", [128, NTT])
        c_b1 = sb("c_b1", [128, H // 128]); c_b2 = sb("c_b2", [128, E])
        m_hst = sb("m_hst", [128, 2, NST])
        r_g1 = sb("r_g1", [128, NTT]); r_g2 = sb("r_g2", [128, NTT])
        r_e1 = sb("r_e1", [128, NTT]); r_e2 = sb("r_e2", [128, NTT])
        r_v1 = sb("r_v1", [128, NTT]); r_v2 = sb("r_v2", [128, NTT])
        r_p1 = sb("r_p1", [128, NTT]); r_p2 = sb("r_p2", [128, NTT])
        r_k1 = sb("r_k1", [128, NTT]); r_k2 = sb("r_k2", [128, NTT])
        r_gk1 = sb("r_gk1", [128, NTT]); r_gk2 = sb("r_gk2", [128, NTT])
        r_sel1 = sb("r_sel1", [128, NTT * E]); r_sel2 = sb("r_sel2", [128, NTT * E])
        r_ohb = sb("r_ohb", [128, NTT * E])
        r_row1 = sb("r_row1", [128, NTT], U32); r_row2 = sb("r_row2", [128, NTT], U32)
        r_d1 = sb("r_d1", [128, NTT], U32); r_d2 = sb("r_d2", [128, NTT], U32)
        r_tu = sb("r_tu", [128, NTT], U32)
        r_tots = sb("r_tots", [NTT, E]); r_cnt = sb("r_cnt", [NCORE, E])
        r_stg = sb("r_stg", [1, NTT * E]); r_c8 = sb("r_c8", [1, E])
        l_s1 = sb("l_s1", [128, 1]); l_q = sb("l_q", [128, 1])
        l_m = sb("l_m", [128, 1]); l_v = sb("l_v", [128, 1])
        l_r = sb("l_r", [128, 1]); l_nb = sb("l_nb", [128, 1])
        l_t = sb("l_t", [128, 1])

        AR = sb("AR", [128, 44160])

        class Arena:
            def __init__(self, t): self.t = t; self.off = 0
            def take(self, words):
                off = self.off
                self.off += -(-words // 128) * 128
                assert self.off <= 44160, f"arena overflow {self.off}"
                return self.t[:, off:off + words]

        ar = Arena(AR)
        a_win = ar.take(8 * 512).rearrange("p (k c) -> p k c", k=8)
        a_xt = ar.take(2 * 512).rearrange("p (k c) -> p k c", k=2)
        a_xi = ar.take(2 * 2 * (TCH + 3)).rearrange("p (cb blk c) -> p cb blk c", cb=2, blk=2)
        a_zs = ar.take(4 * TCH).rearrange("p (cb blk c) -> p cb blk c", cb=2, blk=2)
        a_u = ar.take(4 * TCH).rearrange("p (cb blk c) -> p cb blk c", cb=2, blk=2)
        a_dl = ar.take(4 * TCH).rearrange("p (cb blk c) -> p cb blk c", cb=2, blk=2)
        a_ya = ar.take(4 * TCH).rearrange("p (cb blk c) -> p cb blk c", cb=2, blk=2)
        a_dA = ar.take(6 * TCH).rearrange("p (pn blk c) -> p pn blk c", pn=3, blk=2)
        a_b = ar.take(2 * TCH).rearrange("p (k c) -> p k c", k=2)
        a_B = ar.take(2 * TCH).rearrange("p (k c) -> p k c", k=2)
        a_C = ar.take(2 * TCH).rearrange("p (k c) -> p k c", k=2)
        a_dtr = ar.take(TCH)
        a_ssm33 = stk.enter_context(nc.sbuf_tensor("a_ssm33", [33, TCH], F32))

        ar2 = Arena(AR)
        b_wout = ar2.take(16 * 1024).rearrange("p (k c) -> p k c", k=16)
        b_xf = ar2.take(NTT * 1024).rearrange("p (k c) -> p k c", k=NTT)
        b_ygk = ar2.take(4 * 512).rearrange("p (k c) -> p k c", k=4)
        b_xr = ar2.take(4 * 1024).rearrange("p (k c) -> p k c", k=4)
        b_xfT = ar2.take(8 * 128).rearrange("p (k c) -> p k c", k=8)
        b_lg = ar2.take(NTT * E).rearrange("p (k c) -> p k c", k=NTT)
        b_pb = ar2.take(NTT * E).rearrange("p (k c) -> p k c", k=NTT)
        b_zero = ar2.take(E * SHARD // 128)
        l_scr = ar2.take(D)
        b_lex16 = stk.enter_context(nc.sbuf_tensor("b_lex16", [NTT, NTT * 128], F32))

        ar3 = Arena(AR)
        e_w1 = ar3.take(2 * 1024).bitcast(BF16).rearrange("p (k c) -> p k c", k=2)
        e_w2 = ar3.take(2 * 4096).bitcast(BF16).rearrange("p (k c) -> p k c", k=2)
        e_hTh = ar3.take(32 * 256).bitcast(BF16).rearrange("p (k c) -> p k c", k=32)
        e_hTl = ar3.take(32 * 256).bitcast(BF16).rearrange("p (k c) -> p k c", k=32)
        e_htmp = ar3.take(512)
        e_xgh = ar3.take(8 * 256).bitcast(BF16).rearrange("p (k c) -> p k c", k=8)
        e_xgl = ar3.take(8 * 256).bitcast(BF16).rearrange("p (k c) -> p k c", k=8)
        e_rows = ar3.take(4 * 1024).rearrange("p (k c) -> p k c", k=4)
        e_yT = ar3.take(8 * 512).rearrange("p (k c) -> p k c", k=8)
        e_yr = ar3.take(4 * 1024).rearrange("p (k c) -> p k c", k=4)
        e_tbs = ar3.take(128)
        e_tbt = ar3.take(128)
        e_tb128 = ar3.take(TCOL)
        e_tix = stk.enter_context(nc.sbuf_tensor("e_tix", [128, TCOL], U32))

        ar4 = Arena(AR)
        f_r1 = ar4.take(4 * 1024).rearrange("p (k c) -> p k c", k=4)
        f_r2 = ar4.take(4 * 1024).rearrange("p (k c) -> p k c", k=4)
        f_xf = ar4.take(4 * 1024).rearrange("p (k c) -> p k c", k=4)
        f_cmb = ar4.take(2 * 1024).rearrange("p (k c) -> p k c", k=2)

        pm = [stk.enter_context(nc.psum_tensor(f"pm{i}", [128, 512], F32)) for i in range(8)]

        sch = Sched()
        for name in SEMS:
            sch.reg(name, stk.enter_context(nc.semaphore(name)))
        OP = sch.op

        def SD(dst, src, sem, waits=()):
            return OP("sync", lambda e, d=dst, s=src: e.dma_start(out=d, in_=s),
                      sem=sem, k=16, waits=waits)

        def GD(dst, src, sem, waits=()):
            return OP("gpsimd", lambda e, d=dst, s=src: e.dma_start(out=d, in_=s),
                      sem=sem, k=16, waits=waits)

        def barrier():
            w = sch.all_now()
            for eng in ("sync", "tensor", "vector", "scalar", "gpsimd"):
                OP(eng, lambda e: None, waits=w)

        SD(a_win[:], winT.rearrange("(k p) c -> p k c", p=128), "sW")
        SD(c_xpT[:], xpT, "sW")
        SD(c_dtw[:], dtw, "sW"); SD(c_dtb[:], dtb, "sW")
        SD(c_A[:], alog, "sW"); SD(c_D[:], dpar, "sW")
        SD(c_conv[:], convw, "sW"); SD(c_cb[:], convb, "sW")
        SD(c_wg[:], wg.rearrange("(k p) c -> p k c", p=128), "sW")
        SD(c_id[:], ident, "sW"); SD(c_lex[:], lexcl, "sW")
        SD(c_selrep[:], selrep, "sW")
        SD(c_ones16[:], ones16, "sW"); SD(c_ones128[:], ones128, "sW")
        SD(c_iota8[:], iota8, "sW")
        SD(c_dump1[:], dump1, "sW"); SD(c_dump2[:], dump2, "sW")
        SD(c_tokv[:], tokv, "sW")
        SD(c_b1[:], b1r, "sW"); SD(c_b2[:], b2r, "sW")
        wld = sch.now("sW")
        OP("vector", lambda e: e.memset(m_hst[:], 0.0), sem="vC")
        for _cb in range(2):
            for _bk in range(2):
                OP("vector", lambda e, _cb=_cb, _bk=_bk:
                   e.memset(a_xi[:, _cb, _bk, 0:3], 0.0), sem="vC")
        OP("scalar", lambda e: e.activation(c_A[:], c_A[:], AF.Exp), sem="sC", waits=wld)
        OP("vector", lambda e: e.tensor_scalar_mul(c_A[:], c_A[:], -1.0), sem="vC",
           waits=sch.now("sC"))

        xt_free = [0, 0]
        pm_free = {g: ("sC", 0) for g in range(8)}
        gate_v = [0, 0]
        xi_rd_v = [0, 0]
        u_rd_v = [0, 0]
        dtr_rd_s = [0]
        zs_evac_s = [0, 0]
        silu_s = [0, 0]
        dA_s = {}
        dA_rd_v = [[0, 0] for _ in range(3)]
        thrB = [0, 0]; thrC = [0, 0]
        bfree_v = [0, 0]; cfree_v = [0, 0]

        def pre_inproj(j, f):
            cb = j % 2
            t0 = TCH * j
            if True:
                for k in range(8):
                    buf = k % 2
                    sxt = f"sXT{buf}"
                    thr = SD(a_xt[:, buf, :],
                             xT[128 * k:128 * (k + 1), t0 + 512 * f: t0 + 512 * (f + 1)],
                             sxt, waits=[("tC", xt_free[buf])])
                    wk = [(sxt, thr)]
                    if k == 0:
                        wk = wk + [max((pm_free[g] for g in range(4)), key=lambda t: t[1])]
                        if j == 0 and f == 0:
                            wk = wk + wld
                    for g in range(4):
                        OP("tensor", lambda e, g=g, k=k, buf=buf:
                           e.matmul(pm[g][:, :], a_win[:, k, 128 * g:128 * (g + 1)],
                                    a_xt[:, buf, :], start=(k == 0), stop=(k == 7)),
                           sem="tC", waits=wk if g == 0 else ())
                    xt_free[buf] = sch.ctr["tC"]
                wt = sch.now("tC") + [("vC", max(gate_v[cb], xi_rd_v[cb]))]
                for g in range(4):
                    if g < 2:
                        pm_free[g] = ("sC", OP("scalar", lambda e, g=g, f=f, cb=cb:
                            e.copy(a_xi[:, cb, g, 3 + 512 * f: 3 + 512 * (f + 1)], pm[g][:, :]),
                            sem="sC", waits=wt))
                    else:
                        pm_free[g] = ("sC", OP("scalar", lambda e, g=g, f=f, cb=cb:
                            e.activation(a_zs[:, cb, g - 2, 512 * f: 512 * (f + 1)],
                                         pm[g][:, :], AF.Silu),
                            sem="sC", waits=wt))
            zs_evac_s[cb] = sch.ctr["sC"]

        def pre_conv_xproj(j):
            cb = j % 2
            wexc = sch.now("sC")
            if j > 0:
                for blk in range(2):
                    xi_rd_v[1 - cb] = OP("vector", lambda e, blk=blk, cb=cb:
                        e.tensor_copy(a_xi[:, cb, blk, 0:3],
                                      a_xi[:, 1 - cb, blk, TCH:TCH + 3]),
                        sem="vC", waits=wexc)
            for blk in range(2):
                OP("scalar", lambda e, blk=blk, cb=cb:
                   e.activation(a_u[:, cb, blk, :], a_xi[:, cb, blk, 3:3 + TCH], AF.Identity,
                                scale=c_conv[:, blk, 3:4], bias=c_cb[:, blk:blk + 1]),
                   sem="sC", waits=[("vC", u_rd_v[cb])])
                for k in range(3):
                    tt = OP("vector", lambda e, blk=blk, k=k, cb=cb:
                       e.scalar_tensor_tensor(a_u[:, cb, blk, :], a_xi[:, cb, blk, k:k + TCH],
                                              c_conv[:, blk, k:k + 1], a_u[:, cb, blk, :],
                                              ALU.mult, ALU.add),
                       sem="vC", waits=sch.now("sC"))
                    xi_rd_v[cb] = max(xi_rd_v[cb], tt)
                silu_s[cb] = OP("scalar", lambda e, blk=blk, cb=cb:
                   e.activation(a_u[:, cb, blk, :], a_u[:, cb, blk, :], AF.Silu),
                   sem="sC", waits=sch.now("vC"))
            for f in range(NF):
                wf = [pm_free[4], ("gSSM", 16 * j), ("sC", silu_s[cb])]
                for k in range(2):
                    OP("tensor", lambda e, f=f, k=k, cb=cb:
                       e.matmul(pm[4][0:33, :], c_xpT[:, k, :],
                                a_u[:, cb, k, 512 * f:512 * (f + 1)],
                                start=(k == 0), stop=(k == 1)),
                       sem="tC", waits=wf if k == 0 else ())
                pm_free[4] = ("sC", OP("scalar", lambda e, f=f:
                    e.copy(a_ssm33[:, 512 * f:512 * (f + 1)], pm[4][0:33, :]),
                    sem="sC", waits=sch.now("tC")))
            GD(ssm_in[j][:, :], a_ssm33[:], "gSSM",
               waits=[("sC", sch.ctr["sC"]), ("gSSM", 16 * j)])
            OP("gpsimd", lambda e, j=j:
               e.collective_compute("AllReduce", ALU.add, replica_groups=RG,
                                    ins=[ssm_in[j].opt()], outs=[ssm_out[j].opt()]),
               sem="ccAR", waits=[("gSSM", 16 * (j + 1)), ("ccAR", j)])

        def pre_delta(j):
            cb = j % 2
            SD(a_dtr[:], ssm_out[j][0:1, :].partition_broadcast(128), "sDTR",
               waits=[("ccAR", j + 1), ("sDTR", 16 * j), ("sC", dtr_rd_s[0])])
            wdtr = [("sDTR", 16 * (j + 1))]
            for blk in range(2):
                OP("scalar", lambda e, blk=blk, cb=cb:
                   e.activation(a_dl[:, cb, blk, :], a_dtr[:], AF.Abs,
                                scale=c_dtw[:, blk:blk + 1], bias=c_dtb[:, blk:blk + 1]),
                   sem="sC", waits=wdtr)
                OP("scalar", lambda e, blk=blk, cb=cb:
                   e.activation(a_dl[:, cb, blk, :], a_dl[:, cb, blk, :], AF.Exp, scale=-1.0),
                   sem="sC")
                OP("scalar", lambda e, blk=blk, cb=cb:
                   e.activation(a_dl[:, cb, blk, :], a_dl[:, cb, blk, :], AF.Ln, bias=1.0),
                   sem="sC")
                dtr_rd_s[0] = OP("scalar", lambda e, blk=blk, cb=cb:
                   e.activation(a_ya[:, cb, blk, :], a_dtr[:], AF.Relu,
                                scale=c_dtw[:, blk:blk + 1], bias=c_dtb[:, blk:blk + 1]),
                   sem="sC", waits=[("gYG", 16 * (TCH // 512) * max(0, j - 1)),
                                    ("gYG2", 16 * (TCH // 512) * max(0, j - 1)),
                                    ("vC", gate_v[cb])])

        def pre_omega(j):
            cb = j % 2
            for blk in range(2):
                da = OP("vector", lambda e, blk=blk, cb=cb:
                   e.tensor_add(a_dl[:, cb, blk, :], a_dl[:, cb, blk, :], a_ya[:, cb, blk, :]),
                   sem="vC", waits=sch.now("sC"))
                yi = OP("scalar", lambda e, blk=blk, cb=cb:
                   e.activation(a_ya[:, cb, blk, :], a_u[:, cb, blk, :], AF.Identity,
                                scale=c_D[:, blk:blk + 1]),
                   sem="sC", waits=[("vC", da)])
                OP("vector", lambda e, blk=blk, cb=cb:
                   e.tensor_mul(a_u[:, cb, blk, :], a_dl[:, cb, blk, :], a_u[:, cb, blk, :]),
                   sem="vC", waits=[("sC", yi)])

        def emit_dA(j, n):
            cb = j % 2
            for blk in range(2):
                dA_s[(j, n, blk)] = OP("scalar", lambda e, n=n, blk=blk, cb=cb:
                   e.activation(a_dA[:, n % 3, blk, :], a_dl[:, cb, blk, :], AF.Exp,
                                scale=c_A[:, blk, n:n + 1]),
                   sem="sC", waits=[("vC", dA_rd_v[n % 3][blk])])

        def bc_row(j, row, slot, dst, stage_i, free_v):
            return SD(dst[:, slot, :],
                      ssm_out[j][row:row + 1, :].partition_broadcast(128),
                      f"s{'B' if stage_i == 0 else 'C'}{slot}",
                      waits=[("ccAR", j + 1), ("vC", free_v)])

        def emit_scan(j):
            cb = j % 2
            for n in range(3):
                emit_dA(j, n)
            thrB[0] = bc_row(j, 1, 0, a_B, 0, bfree_v[0])
            thrC[0] = bc_row(j, 1 + NST, 0, a_C, 1, cfree_v[0])
            for n in range(NST):
                nb = n % 2
                if n + 1 < NST:
                    thrB[1 - nb] = bc_row(j, 2 + n, 1 - nb, a_B, 0, bfree_v[1 - nb])
                    thrC[1 - nb] = bc_row(j, 2 + NST + n, 1 - nb, a_C, 1, cfree_v[1 - nb])
                for blk in range(2):
                    bm = OP("vector", lambda e, nb=nb, blk=blk, cb=cb:
                       e.tensor_mul(a_b[:, blk, :], a_u[:, cb, blk, :], a_B[:, nb, :]),
                       sem="vC", waits=([(f"sB{nb}", thrB[nb])] if blk == 0 else ()))
                    bfree_v[nb] = bm
                    if n == NST - 1:
                        u_rd_v[cb] = bm
                    sc = OP("vector", lambda e, n=n, blk=blk:
                       e.tensor_tensor_scan(a_b[:, blk, :], a_dA[:, n % 3, blk, :],
                                            a_b[:, blk, :], m_hst[:, blk, n:n + 1],
                                            ALU.mult, ALU.add),
                       sem="vC", waits=[("sC", dA_s[(j, n, blk)])])
                    dA_rd_v[n % 3][blk] = sc
                    OP("vector", lambda e, n=n, blk=blk:
                       e.tensor_copy(m_hst[:, blk, n:n + 1], a_b[:, blk, TCH - 1:TCH]),
                       sem="vC")
                    cfree_v[nb] = OP("vector", lambda e, nb=nb, blk=blk:
                       e.tensor_mul(a_b[:, blk, :], a_b[:, blk, :], a_C[:, nb, :]),
                       sem="vC", waits=([(f"sC{nb}", thrC[nb])] if blk == 0 else ()))
                    OP("vector", lambda e, blk=blk, cb=cb:
                       e.tensor_add(a_ya[:, cb, blk, :], a_ya[:, cb, blk, :], a_b[:, blk, :]),
                       sem="vC")
                if n + 3 < NST:
                    emit_dA(j, n + 3)
                if j + 1 < NCH:
                    if n == 0:
                        pre_inproj(j + 1, 0)
                    elif n == 3 and NF > 1:
                        pre_inproj(j + 1, 1)
                    elif n == 7:
                        pre_conv_xproj(j + 1)
                    elif n == 10:
                        pre_delta(j + 1)
                    elif n == 13:
                        pre_omega(j + 1)
            for blk in range(2):
                gate_v[cb] = OP("vector", lambda e, blk=blk, cb=cb:
                   e.tensor_mul(a_ya[:, cb, blk, :], a_ya[:, cb, blk, :], a_zs[:, cb, blk, :]),
                   sem="vC", waits=[("sC", zs_evac_s[cb])])
            cpc = TOK // TCH
            g0 = ((j % cpc) * TCH) // 512
            npc = TCH // 512
            ygsem = ("gYG", "gYG2")
            for gg in range(npc):
                for kh in range(DSL // 128):
                    GD(yg_pair[kh][j // cpc, g0 + gg, :, :],
                       a_ya[:, cb, kh, 512 * gg:512 * (gg + 1)], ygsem[kh],
                       waits=([("vC", gate_v[cb]), (ygsem[kh], 16 * npc * j)]
                              if gg == 0 else []))

        for _f in range(NF):
            pre_inproj(0, _f)
        pre_conv_xproj(0)
        pre_delta(0)
        pre_omega(0)
        NSPL = H // 128 + 4 * E
        wsub = []
        for sl in range(H // 128):
            wsub.append((w1s[sl], w1bf[sl].rearrange("p (h c) -> p h c", h=2), 0))
        for sl in range(E):
            for q in range(4):
                wsub.append((w2s[sl][:, 1024 * q:1024 * (q + 1)],
                             w2bf[sl].rearrange("p (h c) -> p h c", h=2), q))
        sv_in = AR[:, 42112:43136]
        sv_hi = AR[:, 43136:43648].bitcast(BF16)
        sv_lo = AR[:, 43648:44160].bitcast(BF16)

        def emit_split(i):
            src, dst, q = wsub[i]
            t_in = GD(sv_in[:, :], src, "sWS", waits=[("sWS", 48 * i)])
            hi_t = OP("gpsimd", lambda e: e.tensor_copy(sv_hi[:, :], sv_in[:, :]),
                      sem="gC", waits=[("sWS", t_in)])
            lo_t = OP("gpsimd", lambda e: e.tensor_sub(sv_lo[:, :], sv_in[:, :], sv_hi[:, :]),
                      sem="gC", waits=[("gC", hi_t)])
            GD(dst[:, 0, 1024 * q:1024 * (q + 1)], sv_hi[:, :], "sWS",
               waits=[("gC", lo_t)])
            GD(dst[:, 1, 1024 * q:1024 * (q + 1)], sv_lo[:, :], "sWS")

        for j in range(NCH):
            emit_scan(j)
            for i in range(j * NSPL // NCH, (j + 1) * NSPL // NCH):
                emit_split(i)

        for kh in range(DSL // 128):
            def _a2a_yg(e, kh=kh):
                with allow_ici_all_to_all():
                    return e.collective_compute("AllToAll", ALU.bypass, replica_groups=RG,
                                                ins=[yg_pair[kh].opt()],
                                                outs=[yg_me[kh].opt()])
            OP("gpsimd", _a2a_yg, sem="ccAG",
               waits=[(("gYG", "gYG2")[kh], 16 * (TCH // 512) * NCH)])

        wnc = [(n, v) for n, v in sch.all_now() if not n.startswith("cc")]
        for eng in ("sync", "tensor", "vector", "scalar", "gpsimd"):
            OP(eng, lambda e: None, waits=wnc)
        SD(b_wout[:], woutT.rearrange("(k p) c -> p k c", p=128), "sWO")
        SD(b_lex16[:], lex16, "sWO")
        wwo = sch.now("sWO")

        def ln_tile(src, dst, extra_waits=()):
            OP("vector", lambda e, s=src: e.reduce_sum(l_s1[:], s, AX.X),
               sem="vC", waits=list(extra_waits) + sch.now("sC"))
            OP("scalar", lambda e, s=src:
               e.activation(l_scr[:], s, AF.Square, accum_out=l_q[:]),
               sem="sC", waits=list(extra_waits) + sch.now("vC"))
            OP("vector", lambda e: e.tensor_scalar_mul(l_m[:], l_s1[:], 1.0 / D), sem="vC")
            OP("vector", lambda e: e.tensor_scalar_mul(l_v[:], l_q[:], 1.0 / D),
               sem="vC", waits=sch.now("sC"))
            OP("vector", lambda e: e.tensor_mul(l_t[:], l_m[:], l_m[:]), sem="vC")
            OP("vector", lambda e: e.tensor_sub(l_v[:], l_v[:], l_t[:]), sem="vC")
            OP("vector", lambda e: e.tensor_scalar_add(l_v[:], l_v[:], EPS), sem="vC")
            OP("scalar", lambda e: e.sqrt(l_v[:], l_v[:]), sem="sC", waits=sch.now("vC"))
            OP("vector", lambda e: e.reciprocal(l_r[:], l_v[:]), sem="vC",
               waits=sch.now("sC"))
            OP("vector", lambda e: e.tensor_mul(l_nb[:], l_m[:], l_r[:]), sem="vC")
            OP("vector", lambda e: e.tensor_scalar_mul(l_nb[:], l_nb[:], -1.0), sem="vC")
            OP("scalar", lambda e, s=src, d=dst:
               e.activation(d, s, AF.Identity, scale=l_r[:], bias=l_nb[:]),
               sem="sC", waits=sch.now("vC"))

        def router_tiles(grp):
            jts = range(4 * grp, 4 * (grp + 1))
            wxf = sch.now("sC")
            for jt in jts:
                for dk in range(8):
                    OP("tensor", lambda e, jt=jt, dk=dk:
                       e.transpose(pm[5][:, 0:128], b_xf[:, jt, 128 * dk:128 * (dk + 1)],
                                   c_id[:]),
                       sem="tC", waits=(wxf if (jt == jts[0] and dk == 0) else []) + [pm_free[5]])
                    pm_free[5] = ("sC", OP("scalar", lambda e, dk=dk:
                        e.copy(b_xfT[:, dk, :], pm[5][:, 0:128]),
                        sem="sC", waits=sch.now("tC")))
                wtr = sch.now("sC") + [pm_free[6]]
                for dk in range(8):
                    OP("tensor", lambda e, jt=jt, dk=dk:
                       e.matmul(pm[6][:, 0:E], b_xfT[:, dk, :], c_wg[:, dk, :],
                                start=(dk == 0), stop=(dk == 7)),
                       sem="tC", waits=wtr if dk == 0 else ())
                pm_free[6] = ("sC", OP("scalar", lambda e, jt=jt:
                    e.copy(b_lg[:, jt, :], pm[6][:, 0:E]),
                    sem="sC", waits=sch.now("tC")))
            wlg = sch.now("sC")
            for jt in jts:
                OP("vector", lambda e, jt=jt: e.reduce_max(r_v1[:, jt:jt + 1], b_lg[:, jt, :], AX.X),
                   sem="vC", waits=wlg if jt == jts[0] else ())
            for jt in jts:
                OP("vector", lambda e, jt=jt:
                   e.tensor_scalar_mul(r_v2[:, jt:jt + 1], r_v1[:, jt:jt + 1], -1.0), sem="vC")
            wneg = sch.now("vC")
            for jt in jts:
                OP("scalar", lambda e, jt=jt:
                   e.activation(b_pb[:, jt, :], b_lg[:, jt, :], AF.Exp,
                                bias=r_v2[:, jt:jt + 1], accum_out=r_p1[:, jt:jt + 1]),
                   sem="sC", waits=wneg if jt == jts[0] else ())
            wexp = sch.now("sC")
            OP("vector", lambda e, grp=grp:
               e.reciprocal(r_p2[:, 4 * grp:4 * (grp + 1)], r_p1[:, 4 * grp:4 * (grp + 1)]),
               sem="vC", waits=wexp)
            wrec = sch.now("vC")
            for jt in jts:
                OP("scalar", lambda e, jt=jt:
                   e.activation(b_pb[:, jt, :], b_pb[:, jt, :], AF.Identity,
                                scale=r_p2[:, jt:jt + 1]),
                   sem="sC", waits=wrec if jt == jts[0] else ())
            wpr = sch.now("sC")
            for jt in jts:
                OP("vector", lambda e, jt=jt:
                   e.reduce_max(r_v1[:, jt:jt + 1], b_pb[:, jt, :], AX.X),
                   sem="vC", waits=wpr if jt == jts[0] else ())
            for jt in jts:
                OP("vector", lambda e, jt=jt:
                   e.tensor_scalar(r_sel1[:, E * jt:E * (jt + 1)], b_pb[:, jt, :],
                                   r_v1[:, jt:jt + 1], None, ALU.is_equal),
                   sem="vC")
            for jt in jts:
                OP("vector", lambda e, jt=jt:
                   e.tensor_scalar(b_lg[:, jt, :], r_sel1[:, E * jt:E * (jt + 1)],
                                   -9.0, 9.0, ALU.mult, ALU.add),
                   sem="vC")
            for jt in jts:
                OP("vector", lambda e, jt=jt:
                   e.tensor_add(b_lg[:, jt, :], b_lg[:, jt, :], c_iota8[:]), sem="vC")
            for jt in jts:
                OP("vector", lambda e, jt=jt:
                   e.tensor_reduce(r_e1[:, jt:jt + 1], b_lg[:, jt, :], AX.X, ALU.min),
                   sem="vC")
            for jt in jts:
                OP("vector", lambda e, jt=jt:
                   e.tensor_scalar(r_sel1[:, E * jt:E * (jt + 1)], c_iota8[:],
                                   r_e1[:, jt:jt + 1], None, ALU.is_equal),
                   sem="vC")
            for jt in jts:
                OP("vector", lambda e, jt=jt:
                   e.scalar_tensor_tensor(b_pb[:, jt, :], r_sel1[:, E * jt:E * (jt + 1)],
                                          -2.0, b_pb[:, jt, :], ALU.mult, ALU.add),
                   sem="vC")
            for jt in jts:
                OP("vector", lambda e, jt=jt:
                   e.reduce_max(r_v2[:, jt:jt + 1], b_pb[:, jt, :], AX.X), sem="vC")
            for jt in jts:
                OP("vector", lambda e, jt=jt:
                   e.tensor_scalar(r_sel2[:, E * jt:E * (jt + 1)], b_pb[:, jt, :],
                                   r_v2[:, jt:jt + 1], None, ALU.is_equal), sem="vC")
            for jt in jts:
                OP("vector", lambda e, jt=jt:
                   e.tensor_scalar(b_lg[:, jt, :], r_sel2[:, E * jt:E * (jt + 1)],
                                   -9.0, 9.0, ALU.mult, ALU.add), sem="vC")
            for jt in jts:
                OP("vector", lambda e, jt=jt:
                   e.tensor_add(b_lg[:, jt, :], b_lg[:, jt, :], c_iota8[:]), sem="vC")
            for jt in jts:
                OP("vector", lambda e, jt=jt:
                   e.tensor_reduce(r_e2[:, jt:jt + 1], b_lg[:, jt, :], AX.X, ALU.min),
                   sem="vC")
            for jt in jts:
                OP("vector", lambda e, jt=jt:
                   e.tensor_scalar(r_sel2[:, E * jt:E * (jt + 1)], c_iota8[:],
                                   r_e2[:, jt:jt + 1], None, ALU.is_equal), sem="vC")


        ygk_free = [0, 0, 0, 0]
        evac_v = {-2: 0, -1: 0}
        rt_evac = {-1: 0}
        for grp in range(NGRP):
            SD(b_xr[:], xrows[512 * grp:512 * (grp + 1), :].rearrange(
                "(k p) c -> p k c", p=128), "sXR",
               waits=[("vC", evac_v[grp - 1]), ("sXR", 16 * grp)])
            korder = [2 * i for i in range(8)] + [2 * i + 1 for i in range(8)]
            for ki, k in enumerate(korder):
                buf = ki % 4
                kh = k % (DSL // 128)
                gthr = SD(b_ygk[:, buf, :],
                          yg_me[kh][k // (DSL // 128), grp, :, :],
                          f"gYGK{buf}",
                          waits=[("tC", ygk_free[buf]), ("ccAG", kh + 1)])
                wk = [(f"gYGK{buf}", gthr)]
                if ki == 0:
                    wk = wk + [("vC", evac_v[grp - 1]), ("sC", rt_evac[grp - 1])] + (
                        wwo if grp == 0 else [])
                for ts4 in range(4):
                    for fh in range(2):
                        OP("tensor", lambda e, k=k, buf=buf, ts4=ts4, fh=fh, ki=ki:
                           e.matmul(pm[ts4 * 2 + fh][:, :],
                                    b_ygk[:, buf, 128 * ts4:128 * (ts4 + 1)],
                                    b_wout[:, k, 512 * fh:512 * (fh + 1)],
                                    start=(ki == 0), stop=(ki == 15)),
                           sem="tC", waits=wk if (ts4 == 0 and fh == 0) else ())
                ygk_free[buf] = sch.ctr["tC"]
            wmm = sch.now("tC") + [("sXR", 16 * (grp + 1))]
            for ts4 in range(4):
                jt = grp * 4 + ts4
                for fh in range(2):
                    OP("vector", lambda e, jt=jt, ts4=ts4, fh=fh:
                       e.tensor_add(b_xf[:, jt, 512 * fh:512 * (fh + 1)],
                                    pm[ts4 * 2 + fh][:, :],
                                    b_xr[:, ts4, 512 * fh:512 * (fh + 1)]),
                       sem="vC", waits=wmm)
            evac_v[grp] = sch.ctr["vC"]
            for ts4 in range(4):
                jt = grp * 4 + ts4
                ln_tile(b_xf[:, jt, :], b_xf[:, jt, :])
                SD(xf_in[512 * grp + 128 * ts4:512 * grp + 128 * (ts4 + 1), :],
                   b_xf[:, jt, :], "sXFIN", waits=sch.now("sC"))
            OP("gpsimd", lambda e, grp=grp:
               e.collective_compute("AllGather", ALU.bypass, replica_groups=RG,
                                    ins=[xf_in[512 * grp:512 * (grp + 1), :].opt()],
                                    outs=[xf_out[512 * NCORE * grp:512 * NCORE * (grp + 1), :].opt()]),
               sem="ccXF", waits=[("sXFIN", 64 * (grp + 1))])
            router_tiles(grp)
            rt_evac[grp] = sch.ctr["sC"]

        OP("vector", lambda e: e.tensor_add(r_g1[:], r_v1[:], r_v2[:]), sem="vC")
        OP("vector", lambda e: e.reciprocal(r_g2[:], r_g1[:]), sem="vC")
        OP("vector", lambda e: e.tensor_mul(r_g1[:], r_v1[:], r_g2[:]), sem="vC")
        OP("vector", lambda e: e.tensor_mul(r_g2[:], r_v2[:], r_g2[:]), sem="vC")
        OP("vector", lambda e: e.tensor_add(r_ohb[:], r_sel1[:], r_sel2[:]), sem="vC")
        wohb = sch.now("vC")
        for jt in range(NTT):
            OP("tensor", lambda e, jt=jt:
               e.matmul(pm[7][0:1, 0:E], c_ones128[:], r_ohb[:, E * jt:E * (jt + 1)],
                        start=True, stop=True),
               sem="tC", waits=(wohb if jt == 0 else []) + [pm_free[7]])
            pm_free[7] = ("sC", OP("scalar", lambda e, jt=jt:
                e.copy(r_stg[0:1, E * jt:E * (jt + 1)], pm[7][0:1, 0:E]),
                sem="sC", waits=sch.now("tC")))
        SD(stg_d.rearrange("j e -> (j e)").unsqueeze(0), r_stg[0:1, :], "sTOT",
           waits=sch.now("sC"))
        SD(r_tots[:], stg_d[:, :], "sTOT", waits=[("sTOT", 16)])
        wtots = [("sTOT", 32)]
        OP("tensor", lambda e:
           e.matmul(pm[7][0:1, 0:E], c_ones16[:], r_tots[:], start=True, stop=True),
           sem="tC", waits=wtots + [pm_free[7]])
        pm_free[7] = ("sC", OP("scalar", lambda e: e.copy(r_c8[:], pm[7][0:1, 0:E]),
                               sem="sC", waits=sch.now("tC")))
        GD(cnt_in[:, :], r_c8[:], "gCNT", waits=[pm_free[7]])
        OP("gpsimd", lambda e:
           e.collective_compute("AllGather", ALU.bypass, replica_groups=RG,
                                ins=[cnt_in.opt()], outs=[cnt_out.opt()]),
           sem="ccCNT", waits=[("gCNT", 16)])
        SD(r_cnt[:], cnt_out[:, :], "sCNT", waits=[("ccCNT", 1)])
        wcnt = [("sCNT", 16)]
        for jt in range(NTT):
            bank = 5 + (jt % 2)
            wp = (wcnt if jt < 2 else []) + [pm_free[bank]]
            OP("tensor", lambda e, jt=jt, bank=bank:
               e.matmul(pm[bank][:, 0:E], c_lex[:], r_ohb[:, E * jt:E * (jt + 1)],
                        start=True, stop=False),
               sem="tC", waits=wp)
            OP("tensor", lambda e, jt=jt, bank=bank:
               e.matmul(pm[bank][:, 0:E], b_lex16[:, 128 * jt:128 * (jt + 1)],
                        r_tots[:], start=False, stop=False),
               sem="tC")
            OP("tensor", lambda e, bank=bank:
               e.matmul(pm[bank][:, 0:E], c_selrep[:], r_cnt[:],
                        start=False, stop=True),
               sem="tC")
            wmm2 = sch.now("tC")
            OP("vector", lambda e, jt=jt, bank=bank:
               e.tensor_mul(b_lg[:, jt, :], r_sel1[:, E * jt:E * (jt + 1)],
                            pm[bank][:, 0:E]),
               sem="vC", waits=wmm2)
            OP("vector", lambda e, jt=jt:
               e.reduce_sum(r_p1[:, jt:jt + 1], b_lg[:, jt, :], AX.X), sem="vC")
            OP("vector", lambda e, jt=jt, bank=bank:
               e.tensor_mul(b_lg[:, jt, :], r_sel2[:, E * jt:E * (jt + 1)],
                            pm[bank][:, 0:E]),
               sem="vC")
            pm_free[bank] = ("vC", OP("vector", lambda e, jt=jt:
                e.reduce_sum(r_p2[:, jt:jt + 1], b_lg[:, jt, :], AX.X), sem="vC"))
        OP("vector", lambda e: e.tensor_scalar(r_k1[:], r_p1[:], float(CAP), None, ALU.is_lt),
           sem="vC")
        OP("vector", lambda e: e.tensor_scalar(r_k2[:], r_p2[:], float(CAP), None, ALU.is_lt),
           sem="vC")
        OP("vector", lambda e: e.tensor_mul(r_gk1[:], r_g1[:], r_k1[:]), sem="vC")
        OP("vector", lambda e: e.tensor_mul(r_gk2[:], r_g2[:], r_k2[:]), sem="vC")
        for (rp, re_, rrow) in ((r_p1, r_e1, r_row1), (r_p2, r_e2, r_row2)):
            OP("vector", lambda e, rp=rp: e.tensor_scalar(l_scr[:, 0:NTT], rp[:],
               float(CAPP - 1), None, ALU.min), sem="vC")
            OP("vector", lambda e: e.tensor_scalar(l_scr[:, NTT:2 * NTT],
               l_scr[:, 0:NTT], 512.0, None, ALU.is_ge), sem="vC")
            for kk in range(2, RB // 4 + 1):
                OP("vector", lambda e, kk=kk: e.tensor_scalar(l_scr[:, 2 * NTT:3 * NTT],
                   l_scr[:, 0:NTT], float(512 * kk), None, ALU.is_ge), sem="vC")
                OP("vector", lambda e: e.tensor_add(l_scr[:, NTT:2 * NTT],
                   l_scr[:, NTT:2 * NTT], l_scr[:, 2 * NTT:3 * NTT]), sem="vC")
            OP("vector", lambda e: e.tensor_scalar(l_scr[:, 2 * NTT:3 * NTT],
               l_scr[:, NTT:2 * NTT], float(RB // 4), None, ALU.is_equal), sem="vC")
            OP("vector", lambda e, re_=re_:
               e.scalar_tensor_tensor(l_scr[:, 3 * NTT:4 * NTT], l_scr[:, NTT:2 * NTT],
                                      3584.0, l_scr[:, 0:NTT], ALU.mult, ALU.add), sem="vC")
            OP("vector", lambda e, re_=re_:
               e.scalar_tensor_tensor(l_scr[:, 3 * NTT:4 * NTT], re_[:], 512.0,
                                      l_scr[:, 3 * NTT:4 * NTT], ALU.mult, ALU.add), sem="vC")
            OP("vector", lambda e: e.tensor_scalar(l_scr[:, NTT:2 * NTT],
               l_scr[:, 2 * NTT:3 * NTT], -1.0, 1.0, ALU.mult, ALU.add), sem="vC")
            OP("vector", lambda e: e.tensor_mul(l_scr[:, 3 * NTT:4 * NTT],
               l_scr[:, 3 * NTT:4 * NTT], l_scr[:, NTT:2 * NTT]), sem="vC")
            LR = 128 * (RB - 4 * (RB // 4))
            C2 = 512 * (RB // 4) * (E - 1)
            OP("vector", lambda e, re_=re_, LR=LR:
               e.scalar_tensor_tensor(l_scr[:, NTT:2 * NTT], re_[:], float(LR),
                                      l_scr[:, 0:NTT], ALU.mult, ALU.add), sem="vC")
            OP("vector", lambda e, C2=C2: e.tensor_scalar_add(l_scr[:, NTT:2 * NTT],
               l_scr[:, NTT:2 * NTT], float(C2)), sem="vC")
            OP("vector", lambda e: e.tensor_mul(l_scr[:, NTT:2 * NTT],
               l_scr[:, NTT:2 * NTT], l_scr[:, 2 * NTT:3 * NTT]), sem="vC")
            OP("vector", lambda e: e.tensor_add(l_scr[:, 3 * NTT:4 * NTT],
               l_scr[:, 3 * NTT:4 * NTT], l_scr[:, NTT:2 * NTT]), sem="vC")
            OP("vector", lambda e, rrow=rrow:
               e.tensor_copy(rrow[:], l_scr[:, 3 * NTT:4 * NTT]), sem="vC")
        for (rp, re_, rk, dmp, rd) in ((r_p1, r_e1, r_k1, c_dump1, r_d1),
                                       (r_p2, r_e2, r_k2, c_dump2, r_d2)):
            OP("vector", lambda e, rp=rp, rk=rk:
               e.tensor_mul(l_scr[:, 0:NTT], rp[:], rk[:]), sem="vC")
            OP("vector", lambda e, rk=rk:
               e.tensor_scalar(l_scr[:, NTT:2 * NTT], rk[:], -1.0, 1.0, ALU.mult, ALU.add),
               sem="vC")
            OP("vector", lambda e, dmp=dmp:
               e.tensor_mul(l_scr[:, NTT:2 * NTT], l_scr[:, NTT:2 * NTT], dmp[:]), sem="vC")
            OP("vector", lambda e:
               e.tensor_add(l_scr[:, 0:NTT], l_scr[:, 0:NTT], l_scr[:, NTT:2 * NTT]),
               sem="vC")
            OP("vector", lambda e, re_=re_:
               e.scalar_tensor_tensor(l_scr[:, 0:NTT], re_[:], float(SHARD),
                                      l_scr[:, 0:NTT], ALU.mult, ALU.add), sem="vC")
            OP("vector", lambda e, rd=rd: e.tensor_copy(rd[:], l_scr[:, 0:NTT]), sem="vC")
        OP("vector", lambda e: e.memset(b_zero[:], 0.0), sem="vC")
        tbl_flat = tbl_in.rearrange("e w -> (e w)")
        GD(tbl_flat.rearrange("(p col) -> p col", p=128), b_zero[:], "gZERO",
           waits=sch.now("vC"))
        wz = [("gZERO", 16)] + sch.now("vC")
        for jt in range(NTT):
            OP("gpsimd", lambda e, jt=jt:
               e.indirect_dma_start(tbl_flat.unsqueeze(1),
                                    IndirectOffsetOnAxis(ap=r_d1[:, jt:jt + 1], axis=0),
                                    c_tokv[:, jt:jt + 1], None),
               sem="gSCAT", k=16, waits=wz if jt == 0 else ())
            OP("gpsimd", lambda e, jt=jt:
               e.indirect_dma_start(tbl_flat.unsqueeze(1),
                                    IndirectOffsetOnAxis(ap=r_d2[:, jt:jt + 1], axis=0),
                                    c_tokv[:, jt:jt + 1], None),
               sem="gSCAT", k=16)

        def _a2a(e):
            with allow_ici_all_to_all():
                return e.collective_compute("AllToAll", ALU.bypass, replica_groups=RG,
                                            ins=[tbl_in.opt()], outs=[tbl_out.opt()])
        OP("gpsimd", _a2a, sem="ccTBL", waits=[("gSCAT", 16 * 2 * NTT)])

        wbar = sch.all_now()
        for _eng in ("sync", "tensor", "vector", "scalar", "gpsimd"):
            OP(_eng, lambda e: None, waits=wbar)
        tb66 = lambda c: tbl_out[c].rearrange("(col p) -> col p", p=128)[0:TCOL, :]
        SD(e_tbs.rearrange('p c -> p c')[0:TCOL, :], tb66(0), "sTB", waits=[("ccTBL", 1)])
        nt = 1
        for c in range(1, NCORE):
            SD(e_tbt.rearrange('p c -> p c')[0:TCOL, :], tb66(c), "sTB",
               waits=sch.now("vC") + [("sTB", 16 * nt)])
            nt += 1
            OP("vector", lambda e: e.tensor_add(e_tbs.rearrange('p c -> p c')[0:TCOL, :], e_tbs.rearrange('p c -> p c')[0:TCOL, :],
                                                e_tbt.rearrange('p c -> p c')[0:TCOL, :]),
               sem="vC", waits=[("sTB", 16 * nt)])
        OP("vector", lambda e: e.tensor_scalar_add(e_tbs.rearrange('p c -> p c')[0:TCOL, :], e_tbs.rearrange('p c -> p c')[0:TCOL, :], -1.0),
           sem="vC")
        OP("vector", lambda e: e.tensor_scalar_max(e_tbs.rearrange('p c -> p c')[0:TCOL, :], e_tbs.rearrange('p c -> p c')[0:TCOL, :], 0.0),
           sem="vC")
        OP("tensor", lambda e:
           e.transpose(pm[6][:, 0:TCOL], e_tbs.rearrange('p c -> p c')[0:TCOL, :], c_id[0:TCOL, 0:TCOL]),
           sem="tC", waits=sch.now("vC") + [pm_free[6]])
        OP("scalar", lambda e: e.copy(e_tb128[:], pm[6][:, 0:TCOL]),
           sem="sC", waits=sch.now("tC"))
        pm_free[6] = ("sC", sch.ctr["sC"])
        OP("vector", lambda e: e.tensor_copy(e_tix[:], e_tb128[:]),
           sem="vC", waits=sch.now("sC"))
        wtix = sch.now("vC")

        w1_free = [0, 0]
        w2_free = [0, 0]
        hsp_free = [0, 0]
        rowg_thr = 0
        NRC = -(-RB // 4)
        ytail = [None]
        for rc in range(NRC):
            nb = min(4, RB - 4 * rc)
            rows = nb * 128
            for b in range(nb):
                rowg_thr = OP("gpsimd", lambda e, rc=rc, b=b:
                    e.indirect_dma_start(e_rows[:, b, :], None, xf_out[:, :],
                                         IndirectOffsetOnAxis(
                                             ap=e_tix[:, 4 * rc + b:4 * rc + b + 1], axis=0)),
                    sem="gROW", k=16,
                    waits=wtix + [("ccXF", NGRP)] + (sch.now("tC") if rc > 0 else []))
            if ytail[0] is not None:
                ytail[0]()
                ytail[0] = None
            wgth = [("gROW", rowg_thr)]
            for b in range(nb):
                for dk in range(8):
                    bank = 6 + ((b * 8 + dk) % 2)
                    OP("tensor", lambda e, b=b, dk=dk, bank=bank:
                       e.transpose(pm[bank][:, 0:128], e_rows[:, b, 128 * dk:128 * (dk + 1)],
                                   c_id[:]),
                       sem="tC", waits=wgth + [pm_free[bank]])
                    hh = OP("scalar", lambda e, b=b, dk=dk, bank=bank:
                        e.copy(e_xgh[:, dk, 128 * b:128 * (b + 1)], pm[bank][:, 0:128]),
                        sem="sC", waits=sch.now("tC"))
                    pm_free[bank] = ("vC", OP("vector", lambda e, b=b, dk=dk, bank=bank:
                        e.tensor_sub(e_xgl[:, dk, 128 * b:128 * (b + 1)],
                                     pm[bank][:, 0:128],
                                     e_xgh[:, dk, 128 * b:128 * (b + 1)]),
                        sem="vC", waits=[("sC", hh)]))
            wxgt = sch.now("vC", "sC")
            for hc in range(32):
                buf = hc % 2
                w1t = SD(e_w1[:, buf, :], w1bf[hc], f"sW1{buf}",
                         waits=[("tC", w1_free[buf]), ("sWS", 48 * NSPL)])
                wkk = [(f"sW1{buf}", w1t)] + (wxgt if hc < 2 else []) + [pm_free[hc % 2]]
                first = True
                for kk in range(8):
                    for (wof, xg) in ((0, e_xgh), (0, e_xgl), (1024, e_xgh)):
                        OP("tensor", lambda e, hc=hc, buf=buf, kk=kk, rows=rows,
                                     wof=wof, xg=xg, first=first:
                           e.matmul(pm[hc % 2][:, 0:rows],
                                    e_w1[:, buf, wof + 128 * kk:wof + 128 * (kk + 1)],
                                    xg[:, kk, 0:rows],
                                    start=first, stop=(kk == 7 and wof == 1024)),
                           sem="tC", waits=wkk if first else ())
                        first = False
                w1_free[buf] = sch.ctr["tC"]
                g1t = OP("scalar", lambda e, hc=hc, rows=rows:
                    e.activation(e_htmp[:, 0:rows], pm[hc % 2][:, 0:rows],
                                 AF.Gelu_apprx_tanh, bias=c_b1[:, hc:hc + 1]),
                    sem="sC", waits=sch.now("tC") + [("vC", hsp_free[hc % 2])])
                pm_free[hc % 2] = ("sC", g1t)
                g2t = OP("scalar", lambda e, hc=hc, rows=rows:
                    e.copy(e_hTh[:, hc, 0:rows], e_htmp[:, 0:rows]),
                    sem="sC")
                hsp_free[hc % 2] = OP("vector", lambda e, hc=hc, rows=rows:
                    e.tensor_sub(e_hTl[:, hc, 0:rows], e_htmp[:, 0:rows],
                                 e_hTh[:, hc, 0:rows]),
                    sem="vC", waits=[("sC", g2t)])
            whT = sch.now("sC", "vC")
            for dc in range(8):
                buf = dc % 2
                w2t = SD(e_w2[:, buf, :], w2bf[dc], f"sW2{buf}",
                         waits=[("tC", w2_free[buf]), ("sWS", 48 * NSPL)])
                wkk = [(f"sW2{buf}", w2t)] + (whT if dc < 2 else []) + [pm_free[2 + dc % 2]]
                first = True
                for hk in range(32):
                    for (wof, ht) in ((0, e_hTh), (0, e_hTl), (4096, e_hTh)):
                        OP("tensor", lambda e, dc=dc, buf=buf, hk=hk, rows=rows,
                                     wof=wof, ht=ht, first=first:
                           e.matmul(pm[2 + dc % 2][:, 0:rows],
                                    e_w2[:, buf, wof + 128 * hk:wof + 128 * (hk + 1)],
                                    ht[:, hk, 0:rows],
                                    start=first, stop=(hk == 31 and wof == 4096)),
                           sem="tC", waits=wkk if first else ())
                        first = False
                w2_free[buf] = sch.ctr["tC"]
                pm_free[2 + dc % 2] = ("sC", OP("scalar", lambda e, dc=dc, rows=rows:
                    e.activation(e_yT[:, dc, 0:rows], pm[2 + dc % 2][:, 0:rows],
                                 AF.Identity, bias=c_b2[:, dc:dc + 1]),
                    sem="sC", waits=sch.now("tC")))
            wyT = sch.now("sC")
            for dc in range(8):
                for b in range(nb):
                    bank = 4 + ((dc * nb + b) % 2)
                    OP("tensor", lambda e, dc=dc, b=b, bank=bank:
                       e.transpose(pm[bank][:, 0:128], e_yT[:, dc, 128 * b:128 * (b + 1)],
                                   c_id[:]),
                       sem="tC", waits=wyT + [pm_free[bank]] +
                       ([("gYIN", 16 * rc)] if dc == 0 and b == 0 else []))
                    pm_free[bank] = ("sC", OP("scalar", lambda e, dc=dc, b=b, bank=bank:
                        e.copy(e_yr[:, b, 128 * dc:128 * (dc + 1)], pm[bank][:, 0:128]),
                        sem="sC", waits=sch.now("tC")))
            ysc = sch.now("sC")

            def emit_ytail(rc=rc, rows=rows, nb=nb, ysc=ysc):
                GD(y_in[512 * rc:512 * rc + rows, :].rearrange("(b p) c -> p b c", p=128),
                   e_yr[:, 0:nb, :], "gYIN", waits=ysc + [("gYIN", 16 * rc)])
                off = E * 512 * rc
                OP("gpsimd", lambda e, rc=rc, rows=rows, off=off:
                   e.collective_compute("AllGather", ALU.bypass, replica_groups=RG,
                                        ins=[y_in[512 * rc:512 * rc + rows, :].opt()],
                                        outs=[y_out[off:off + E * rows, :].opt()]),
                   sem="ccY", waits=[("gYIN", 16 * (rc + 1))])
            ytail[0] = emit_ytail
        ytail[0]()

        barrier()
        gthr = {}; fxthr = {}
        cons_v = {-4: 0, -3: 0, -2: 0, -1: 0}; cons_s = {-4: 0, -3: 0, -2: 0, -1: 0}
        fxsem = ("sXF4", "sXF5", "sXF6", "sXF7")

        def issue_g4(jt):
            b = jt % 4
            OP("gpsimd", lambda e, jt=jt, b=b:
                e.indirect_dma_start(f_r1[:, b, :], None, y_out[:, :],
                                     IndirectOffsetOnAxis(ap=r_row1[:, jt:jt + 1], axis=0)),
                sem=f"gR4{b}", k=16,
                waits=[("ccY", NRC), ("sC", cons_s[jt - 4]), ("vC", cons_v[jt - 4])])
            gthr[jt] = OP("gpsimd", lambda e, jt=jt, b=b:
                e.indirect_dma_start(f_r2[:, b, :], None, y_out[:, :],
                                     IndirectOffsetOnAxis(ap=r_row2[:, jt:jt + 1], axis=0)),
                sem=f"gR4{b}", k=16)
            fxthr[jt] = SD(f_xf[:, b, :], xf_in[128 * jt:128 * (jt + 1), :], fxsem[b],
                           waits=[("vC", cons_v[jt - 4])])

        for j0 in range(min(4, NTT)):
            issue_g4(j0)
        for jt in range(NTT):
            b = jt % 4
            bc = jt % 2
            cons_s[jt] = OP("scalar", lambda e, jt=jt, b=b, bc=bc:
               e.activation(f_cmb[:, bc, :], f_r1[:, b, :], AF.Identity,
                            scale=r_gk1[:, jt:jt + 1]),
               sem="sC", waits=[(f"gR4{b}", gthr[jt]), ("sOUT", 16 * jt)])
            OP("vector", lambda e, jt=jt, b=b, bc=bc:
               e.scalar_tensor_tensor(f_cmb[:, bc, :], f_r2[:, b, :], r_gk2[:, jt:jt + 1],
                                      f_cmb[:, bc, :], ALU.mult, ALU.add),
               sem="vC", waits=sch.now("sC"))
            OP("vector", lambda e, jt=jt, b=b, bc=bc:
               e.tensor_add(f_cmb[:, bc, :], f_cmb[:, bc, :], f_xf[:, b, :]), sem="vC",
               waits=[(fxsem[b], fxthr[jt])])
            ln_tile(f_cmb[:, bc, :], f_cmb[:, bc, :])
            cons_v[jt] = sch.ctr["vC"]
            cons_s[jt] = sch.ctr["sC"]
            if jt + 4 < NTT:
                issue_g4(jt + 4)
            SD(out[128 * jt:128 * (jt + 1), :], f_cmb[:, bc, :], "sOUT",
               waits=sch.now("sC"))
        OP("sync", lambda e: None, waits=[("sOUT", 16 * NTT)])
        OP("gpsimd", lambda e: None, waits=[("sOUT", 16 * NTT)])

        with nc.Block() as block:
            @block.sync
            def _(e):
                for eng, fn in sch.items:
                    if eng == "sync":
                        fn(e)

            @block.tensor
            def _(e):
                for eng, fn in sch.items:
                    if eng == "tensor":
                        fn(e)

            @block.vector
            def _(e):
                for eng, fn in sch.items:
                    if eng == "vector":
                        fn(e)

            @block.scalar
            def _(e):
                for eng, fn in sch.items:
                    if eng == "scalar":
                        fn(e)

            @block.gpsimd
            def _(e):
                for eng, fn in sch.items:
                    if eng == "gpsimd":
                        fn(e)

    return nc


_NC_CACHE = {}

def _get_nc():
    if "nc" not in _NC_CACHE:
        _NC_CACHE["nc"] = build()
    return _NC_CACHE["nc"]


def kernel(L0_attn_x, L0_attn_in_proj_w, L0_attn_conv_w, L0_attn_conv_b,
           L0_attn_x_proj_w, L0_attn_dt_w, L0_attn_dt_b, L0_attn_A_log,
           L0_attn_D_param, L0_attn_out_proj_w, L0_ffn_Wg, L0_ffn_W1, L0_ffn_b1,
           L0_ffn_W2, L0_ffn_b2):
    x = np.ascontiguousarray(np.asarray(L0_attn_x, dtype=np.float32).reshape(T, D))
    xTh = np.ascontiguousarray(x.T)
    Win = np.asarray(L0_attn_in_proj_w, dtype=np.float32)
    convw_h = np.ascontiguousarray(np.asarray(L0_attn_conv_w, dtype=np.float32)[:, 0, :])
    convb_h = np.asarray(L0_attn_conv_b, dtype=np.float32).reshape(DI, 1)
    xpw = np.asarray(L0_attn_x_proj_w, dtype=np.float32)
    dtw_h = np.asarray(L0_attn_dt_w, dtype=np.float32).reshape(DI, 1)
    dtb_h = np.asarray(L0_attn_dt_b, dtype=np.float32).reshape(DI, 1)
    alog_h = np.asarray(L0_attn_A_log, dtype=np.float32)
    dpar_h = np.asarray(L0_attn_D_param, dtype=np.float32).reshape(DI, 1)
    woutT_h = np.ascontiguousarray(np.asarray(L0_attn_out_proj_w, dtype=np.float32).T)
    wg_h = np.asarray(L0_ffn_Wg, dtype=np.float32)
    W1 = np.asarray(L0_ffn_W1, dtype=np.float32)
    b1 = np.asarray(L0_ffn_b1, dtype=np.float32)
    W2 = np.asarray(L0_ffn_W2, dtype=np.float32)
    b2 = np.asarray(L0_ffn_b2, dtype=np.float32)

    lane = np.arange(128)
    ident_h = np.eye(128, dtype=np.float32)
    lexcl_h = (lane[:, None] < lane[None, :]).astype(np.float32)
    lex16_h = np.zeros((NTT, NTT * 128), np.float32)
    for jj in range(NTT):
        lex16_h[:jj, 128 * jj:128 * (jj + 1)] = 1.0
    ones16_h = np.ones((NTT, 1), np.float32)
    ones128_h = np.ones((128, 1), np.float32)
    iota8_h = np.broadcast_to(np.arange(E, dtype=np.float32), (128, E)).copy()
    dump1_h = np.empty((128, NTT), np.float32)
    dump2_h = np.empty((128, NTT), np.float32)
    for jj in range(NTT):
        dump1_h[:, jj] = CAPP + 2 * (128 * jj + lane)
        dump2_h[:, jj] = CAPP + 2 * (128 * jj + lane) + 1

    in_maps = []
    for c in range(NCORE):
        dsl = slice(DSL * c, DSL * (c + 1))
        zsl = slice(DI + DSL * c, DI + DSL * (c + 1))
        winT_c = np.ascontiguousarray(np.concatenate([Win[dsl], Win[zsl]], axis=0).T)
        tokv_h = np.empty((128, NTT), np.float32)
        for jj in range(NTT):
            gg = jj // 4
            q = 128 * jj + lane - 512 * gg
            tokv_h[:, jj] = 512 * NCORE * gg + 512 * c + q + 1

        selrep_h = np.zeros((E, 128), np.float32)
        selrep_h[:c, :] = 1.0
        w1s_h = np.ascontiguousarray(
            W1[c].reshape(8, 128, 32, 128).transpose(2, 1, 0, 3).reshape(32, 128, 1024))
        w2s_h = np.ascontiguousarray(
            W2[c].reshape(32, 128, 8, 128).transpose(2, 1, 0, 3).reshape(8, 128, 4096))
        b1r_h = np.ascontiguousarray(b1[c].reshape(32, 128).T)
        b2r_h = np.ascontiguousarray(b2[c].reshape(8, 128).T)
        in_maps.append(dict(
            xT=xTh, xrows=np.ascontiguousarray(x[TOK * c:TOK * (c + 1)]),
            winT=winT_c,
            convw=np.ascontiguousarray(convw_h[dsl].reshape(2, 128, DC).transpose(1, 0, 2)),
            convb=np.ascontiguousarray(convb_h[dsl].reshape(2, 128).transpose(1, 0)),
            xpT=np.ascontiguousarray(xpw[:, dsl].T.reshape(2, 128, 33).transpose(1, 0, 2)),
            dtw=np.ascontiguousarray(dtw_h[dsl].reshape(2, 128).transpose(1, 0)),
            dtb=np.ascontiguousarray(dtb_h[dsl].reshape(2, 128).transpose(1, 0)),
            alog=np.ascontiguousarray(alog_h[dsl].reshape(2, 128, NST).transpose(1, 0, 2)),
            dpar=np.ascontiguousarray(dpar_h[dsl].reshape(2, 128).transpose(1, 0)),
            woutT=woutT_h, wg=wg_h, w1s=w1s_h, w2s=w2s_h, b1r=b1r_h, b2r=b2r_h,
            ident=ident_h, lexcl=lexcl_h, lex16=lex16_h, selrep=selrep_h,
            ones16=ones16_h, ones128=ones128_h, iota8=iota8_h,
            dump1=dump1_h, dump2=dump2_h, tokv=tokv_h,
        ))

    nc_prog = _get_nc()
    res = run_bass_kernel_spmd(nc_prog, in_maps, core_ids=list(range(NCORE)))
    outp = np.concatenate([np.asarray(res.results[c]["out"]) for c in range(NCORE)], axis=0)
    return outp.reshape(1, T, D).astype(np.float32)


# version 26
# speedup vs baseline: 1.0098x; 1.0098x over previous
import contextlib
import numpy as np
import concourse.bass as bass
import concourse.mybir as mybir
from concourse.bass_utils import run_bass_kernel_spmd
from concourse.bass_core_rust import IndirectOffsetOnAxis
from concourse.aot_env import allow_ici_all_to_all
from concourse import library_config

F32 = mybir.dt.float32
U32 = mybir.dt.uint32
BF16 = mybir.dt.bfloat16
AF = mybir.ActivationFunctionType
ALU = mybir.AluOpType
AX = mybir.AxisListType

T, D, DI, NST, DC, E, H = 16384, 1024, 2048, 16, 4, 8, 4096
CAP, CAPP = 4301, 4352
NCORE = 8
TCH = 1024
EPS = 1e-5
RG = [list(range(NCORE))]

def _derive():
    g = globals()
    g["SHARD"] = CAPP + 2 * (T // NCORE)
    g["NCH"] = T // TCH
    g["DSL"] = DI // NCORE
    g["TOK"] = T // NCORE
    g["NTT"] = (T // NCORE) // 128
    g["NF"] = TCH // 512
    g["RB"] = CAPP // 128
    g["TCOL"] = (CAPP + 2 * (T // NCORE)) // 128
    g["NGRP"] = (T // NCORE) // 512
_derive()


class Sched:
    def __init__(self):
        self.items = []
        self.ctr = {}
        self.sems = {}

    def reg(self, name, handle):
        self.sems[name] = handle
        self.ctr[name] = 0

    def op(self, eng, make, sem=None, k=1, waits=()):
        waits = list(waits)
        if eng == "vector" and sem == "vC":
            waits.append(("vC", self.ctr["vC"]))
        elif eng == "scalar" and sem == "sC":
            waits.append(("sC", self.ctr["sC"]))
        if sem is not None:
            self.ctr[sem] += k
        tgt = self.ctr[sem] if sem is not None else None
        waits = tuple((n, v) for n, v in waits if v > 0)

        def fn(e, make=make, sem=sem, k=k, waits=waits):
            for wname, wval in waits:
                e.wait_ge(self.sems[wname], wval)
            ins = make(e)
            if ins is not None and sem is not None:
                ins.then_inc(self.sems[sem], k)
        self.items.append((eng, fn))
        return tgt

    def now(self, *names):
        return [(n, self.ctr[n]) for n in names]

    def all_now(self):
        return [(n, v) for n, v in self.ctr.items() if v > 0]


def build():
    nc = bass.Bass("TRN2", target_bir_lowering=False, detect_race_conditions=False)

    def din(name, shape, dt=F32):
        return nc.dram_tensor(name, shape, dt, kind="ExternalInput").ap()

    xT = din("xT", [D, T])
    winT = din("winT", [D, 2 * DSL])
    convw = din("convw", [128, 2, DC])
    convb = din("convb", [128, 2])
    xpT = din("xpT", [128, 2, 33])
    dtw = din("dtw", [128, 2])
    dtb = din("dtb", [128, 2])
    alog = din("alog", [128, 2, NST])
    dpar = din("dpar", [128, 2])
    woutT = din("woutT", [DI, D])
    wg = din("wg", [D, E])
    w1s = din("w1s", [H // 128, 128, 8 * 128])
    w2s = din("w2s", [E, 128, H])
    b1r = din("b1r", [128, H // 128])
    b2r = din("b2r", [128, E])
    xrows = din("xrows", [TOK, D])
    ident = din("ident", [128, 128])
    lexcl = din("lexcl", [128, 128])
    lex16 = din("lex16", [NTT, NTT * 128])
    selrep = din("selrep", [E, 128])
    ones16 = din("ones16", [NTT, 1])
    ones128 = din("ones128", [128, 1])
    iota8 = din("iota8", [128, E])
    dump1 = din("dump1", [128, NTT])
    dump2 = din("dump2", [128, NTT])
    tokv = din("tokv", [128, NTT])

    out = nc.dram_tensor("out", [TOK, D], F32, kind="ExternalOutput").ap()

    ssm_in = [nc.dram_tensor(f"ssm_in{j}", [33, TCH], F32).ap() for j in range(NCH)]
    ssm_out = [nc.dram_tensor(f"ssm_out{j}", [33, TCH], F32).ap() for j in range(NCH)]
    yg_pair = [nc.dram_tensor(f"yg_pair{h}", [NCORE, TOK // 512, 128, 512], F32).ap()
               for h in range(DSL // 128)]
    yg_me = [nc.dram_tensor(f"yg_me{h}", [NCORE, TOK // 512, 128, 512], F32).ap()
             for h in range(DSL // 128)]
    cnt_in = nc.dram_tensor("cnt_in", [1, E], F32).ap()
    cnt_out = nc.dram_tensor("cnt_out", [NCORE, E], F32).ap()
    tbl_in = nc.dram_tensor("tbl_in", [E, SHARD], F32).ap()
    tbl_out = nc.dram_tensor("tbl_out", [E, SHARD], F32).ap()
    xf_in = nc.dram_tensor("xf_in", [TOK, D], F32).ap()
    xf_out = nc.dram_tensor("xf_out", [T, D], F32).ap()
    y_in = nc.dram_tensor("y_in", [CAPP, D], F32).ap()
    y_out = nc.dram_tensor("y_out", [E * CAPP, D], F32).ap()
    stg_d = nc.dram_tensor("stg_d", [NTT, E], F32).ap()
    w1bf = nc.dram_tensor("w1bf", [H // 128, 128, 2 * 8 * 128], BF16).ap()
    w2bf = nc.dram_tensor("w2bf", [E, 128, 2 * H], BF16).ap()

    SEMS = ("tC vC sC gC "
            "sW sXT0 sXT1 sDTR sB0 sB1 sC0 sC1 sWO sXR sTOT sCNT sTB sXF4 sXF5 sXF6 sXF7 sXFIN sOUT "
            "sW10 sW11 sW20 sW21 sWS "
            "gSSM gYG gYG2 gYGK0 gYGK1 gYGK2 gYGK3 gZERO gSCAT gCNT gROW gYIN gR40 gR41 gR42 gR43 "
            "ccAR ccAG ccXF ccCNT ccTBL ccY").split()

    with contextlib.ExitStack() as stk:
        def sb(name, shape, dt=F32):
            return stk.enter_context(nc.sbuf_tensor(name, shape, dt))

        c_xpT = sb("c_xpT", [128, 2, 33])
        c_dtw = sb("c_dtw", [128, 2]); c_dtb = sb("c_dtb", [128, 2])
        c_A = sb("c_A", [128, 2, NST]); c_D = sb("c_D", [128, 2])
        c_conv = sb("c_conv", [128, 2, DC]); c_cb = sb("c_cb", [128, 2])
        c_wg = sb("c_wg", [128, 8, E])
        c_id = sb("c_id", [128, 128]); c_lex = sb("c_lex", [128, 128])
        c_selrep = sb("c_selrep", [E, 128])
        c_ones16 = sb("c_ones16", [NTT, 1]); c_ones128 = sb("c_ones128", [128, 1])
        c_iota8 = sb("c_iota8", [128, E])
        c_dump1 = sb("c_dump1", [128, NTT]); c_dump2 = sb("c_dump2", [128, NTT])
        c_tokv = sb("c_tokv", [128, NTT])
        c_b1 = sb("c_b1", [128, H // 128]); c_b2 = sb("c_b2", [128, E])
        m_hst = sb("m_hst", [128, 2, NST])
        r_g1 = sb("r_g1", [128, NTT]); r_g2 = sb("r_g2", [128, NTT])
        r_e1 = sb("r_e1", [128, NTT]); r_e2 = sb("r_e2", [128, NTT])
        r_v1 = sb("r_v1", [128, NTT]); r_v2 = sb("r_v2", [128, NTT])
        r_p1 = sb("r_p1", [128, NTT]); r_p2 = sb("r_p2", [128, NTT])
        r_k1 = sb("r_k1", [128, NTT]); r_k2 = sb("r_k2", [128, NTT])
        r_gk1 = sb("r_gk1", [128, NTT]); r_gk2 = sb("r_gk2", [128, NTT])
        r_sel1 = sb("r_sel1", [128, NTT * E]); r_sel2 = sb("r_sel2", [128, NTT * E])
        r_ohb = sb("r_ohb", [128, NTT * E])
        r_row1 = sb("r_row1", [128, NTT], U32); r_row2 = sb("r_row2", [128, NTT], U32)
        r_d1 = sb("r_d1", [128, NTT], U32); r_d2 = sb("r_d2", [128, NTT], U32)
        r_tu = sb("r_tu", [128, NTT], U32)
        r_tots = sb("r_tots", [NTT, E]); r_cnt = sb("r_cnt", [NCORE, E])
        r_stg = sb("r_stg", [1, NTT * E]); r_c8 = sb("r_c8", [1, E])
        l_s1 = sb("l_s1", [128, 1]); l_q = sb("l_q", [128, 1])
        l_m = sb("l_m", [128, 1]); l_v = sb("l_v", [128, 1])
        l_r = sb("l_r", [128, 1]); l_nb = sb("l_nb", [128, 1])
        l_t = sb("l_t", [128, 1])

        AR = sb("AR", [128, 44160])

        class Arena:
            def __init__(self, t): self.t = t; self.off = 0
            def take(self, words):
                off = self.off
                self.off += -(-words // 128) * 128
                assert self.off <= 44160, f"arena overflow {self.off}"
                return self.t[:, off:off + words]

        ar = Arena(AR)
        a_win = ar.take(8 * 512).rearrange("p (k c) -> p k c", k=8)
        a_xt = ar.take(2 * 512).rearrange("p (k c) -> p k c", k=2)
        a_xi = ar.take(2 * 2 * (TCH + 3)).rearrange("p (cb blk c) -> p cb blk c", cb=2, blk=2)
        a_zs = ar.take(4 * TCH).rearrange("p (cb blk c) -> p cb blk c", cb=2, blk=2)
        a_u = ar.take(4 * TCH).rearrange("p (cb blk c) -> p cb blk c", cb=2, blk=2)
        a_dl = ar.take(4 * TCH).rearrange("p (cb blk c) -> p cb blk c", cb=2, blk=2)
        a_ya = ar.take(4 * TCH).rearrange("p (cb blk c) -> p cb blk c", cb=2, blk=2)
        a_dA = ar.take(6 * TCH).rearrange("p (pn blk c) -> p pn blk c", pn=3, blk=2)
        a_b = ar.take(2 * TCH).rearrange("p (k c) -> p k c", k=2)
        a_B = ar.take(2 * TCH).rearrange("p (k c) -> p k c", k=2)
        a_C = ar.take(2 * TCH).rearrange("p (k c) -> p k c", k=2)
        a_dtr = ar.take(TCH)
        a_ssm33 = stk.enter_context(nc.sbuf_tensor("a_ssm33", [33, TCH], F32))

        ar2 = Arena(AR)
        b_wout = ar2.take(16 * 1024).rearrange("p (k c) -> p k c", k=16)
        b_xf = ar2.take(NTT * 1024).rearrange("p (k c) -> p k c", k=NTT)
        b_ygk = ar2.take(4 * 512).rearrange("p (k c) -> p k c", k=4)
        b_xr = ar2.take(4 * 1024).rearrange("p (k c) -> p k c", k=4)
        b_xfT = ar2.take(8 * 128).rearrange("p (k c) -> p k c", k=8)
        b_lg = ar2.take(NTT * E).rearrange("p (k c) -> p k c", k=NTT)
        b_pb = ar2.take(NTT * E).rearrange("p (k c) -> p k c", k=NTT)
        b_zero = ar2.take(E * SHARD // 128)
        l_scr = ar2.take(D)
        b_lex16 = stk.enter_context(nc.sbuf_tensor("b_lex16", [NTT, NTT * 128], F32))

        ar3 = Arena(AR)
        e_w1 = ar3.take(2 * 1024).bitcast(BF16).rearrange("p (k c) -> p k c", k=2)
        e_w2 = ar3.take(2 * 4096).bitcast(BF16).rearrange("p (k c) -> p k c", k=2)
        e_hTh = ar3.take(32 * 256).bitcast(BF16).rearrange("p (k c) -> p k c", k=32)
        e_hTl = ar3.take(32 * 256).bitcast(BF16).rearrange("p (k c) -> p k c", k=32)
        e_htmp = ar3.take(512)
        e_xgh = ar3.take(8 * 256).bitcast(BF16).rearrange("p (k c) -> p k c", k=8)
        e_xgl = ar3.take(8 * 256).bitcast(BF16).rearrange("p (k c) -> p k c", k=8)
        e_rows = ar3.take(4 * 1024).rearrange("p (k c) -> p k c", k=4)
        e_yT = ar3.take(8 * 512).rearrange("p (k c) -> p k c", k=8)
        e_yr = ar3.take(4 * 1024).rearrange("p (k c) -> p k c", k=4)
        e_tbs = ar3.take(128)
        e_tbt = ar3.take(128)
        e_tb128 = ar3.take(TCOL)
        e_tix = stk.enter_context(nc.sbuf_tensor("e_tix", [128, TCOL], U32))

        ar4 = Arena(AR)
        f_r1 = ar4.take(4 * 1024).rearrange("p (k c) -> p k c", k=4)
        f_r2 = ar4.take(4 * 1024).rearrange("p (k c) -> p k c", k=4)
        f_xf = ar4.take(4 * 1024).rearrange("p (k c) -> p k c", k=4)
        f_cmb = ar4.take(2 * 1024).rearrange("p (k c) -> p k c", k=2)

        pm = [stk.enter_context(nc.psum_tensor(f"pm{i}", [128, 512], F32)) for i in range(8)]

        sch = Sched()
        for name in SEMS:
            sch.reg(name, stk.enter_context(nc.semaphore(name)))
        OP = sch.op

        def SD(dst, src, sem, waits=()):
            return OP("sync", lambda e, d=dst, s=src: e.dma_start(out=d, in_=s),
                      sem=sem, k=16, waits=waits)

        def GD(dst, src, sem, waits=()):
            return OP("gpsimd", lambda e, d=dst, s=src: e.dma_start(out=d, in_=s),
                      sem=sem, k=16, waits=waits)

        def barrier():
            w = sch.all_now()
            for eng in ("sync", "tensor", "vector", "scalar", "gpsimd"):
                OP(eng, lambda e: None, waits=w)

        SD(a_win[:], winT.rearrange("(k p) c -> p k c", p=128), "sW")
        SD(c_xpT[:], xpT, "sW")
        SD(c_dtw[:], dtw, "sW"); SD(c_dtb[:], dtb, "sW")
        SD(c_A[:], alog, "sW"); SD(c_D[:], dpar, "sW")
        SD(c_conv[:], convw, "sW"); SD(c_cb[:], convb, "sW")
        SD(c_wg[:], wg.rearrange("(k p) c -> p k c", p=128), "sW")
        SD(c_id[:], ident, "sW"); SD(c_lex[:], lexcl, "sW")
        SD(c_selrep[:], selrep, "sW")
        SD(c_ones16[:], ones16, "sW"); SD(c_ones128[:], ones128, "sW")
        SD(c_iota8[:], iota8, "sW")
        SD(c_dump1[:], dump1, "sW"); SD(c_dump2[:], dump2, "sW")
        SD(c_tokv[:], tokv, "sW")
        SD(c_b1[:], b1r, "sW"); SD(c_b2[:], b2r, "sW")
        wld = sch.now("sW")
        OP("vector", lambda e: e.memset(m_hst[:], 0.0), sem="vC")
        for _cb in range(2):
            for _bk in range(2):
                OP("vector", lambda e, _cb=_cb, _bk=_bk:
                   e.memset(a_xi[:, _cb, _bk, 0:3], 0.0), sem="vC")
        OP("scalar", lambda e: e.activation(c_A[:], c_A[:], AF.Exp), sem="sC", waits=wld)
        OP("vector", lambda e: e.tensor_scalar_mul(c_A[:], c_A[:], -1.0), sem="vC",
           waits=sch.now("sC"))

        xt_free = [0, 0]
        pm_free = {g: ("sC", 0) for g in range(8)}
        gate_v = [0, 0]
        xi_rd_v = [0, 0]
        u_rd_v = [0, 0]
        dtr_rd_s = [0]
        zs_evac_s = [0, 0]
        silu_s = [0, 0]
        dA_s = {}
        dA_rd_v = [[0, 0] for _ in range(3)]
        thrB = [0, 0]; thrC = [0, 0]
        bfree_v = [0, 0]; cfree_v = [0, 0]

        def pre_inproj(j, f):
            cb = j % 2
            t0 = TCH * j
            if True:
                for k in range(8):
                    buf = k % 2
                    sxt = f"sXT{buf}"
                    thr = SD(a_xt[:, buf, :],
                             xT[128 * k:128 * (k + 1), t0 + 512 * f: t0 + 512 * (f + 1)],
                             sxt, waits=[("tC", xt_free[buf])])
                    wk = [(sxt, thr)]
                    if k == 0:
                        wk = wk + [max((pm_free[g] for g in range(4)), key=lambda t: t[1])]
                        if j == 0 and f == 0:
                            wk = wk + wld
                    for g in range(4):
                        OP("tensor", lambda e, g=g, k=k, buf=buf:
                           e.matmul(pm[g][:, :], a_win[:, k, 128 * g:128 * (g + 1)],
                                    a_xt[:, buf, :], start=(k == 0), stop=(k == 7)),
                           sem="tC", waits=wk if g == 0 else ())
                    xt_free[buf] = sch.ctr["tC"]
                wt = sch.now("tC") + [("vC", max(gate_v[cb], xi_rd_v[cb]))]
                for g in range(4):
                    if g < 2:
                        pm_free[g] = ("sC", OP("scalar", lambda e, g=g, f=f, cb=cb:
                            e.copy(a_xi[:, cb, g, 3 + 512 * f: 3 + 512 * (f + 1)], pm[g][:, :]),
                            sem="sC", waits=wt))
                    else:
                        pm_free[g] = ("sC", OP("scalar", lambda e, g=g, f=f, cb=cb:
                            e.activation(a_zs[:, cb, g - 2, 512 * f: 512 * (f + 1)],
                                         pm[g][:, :], AF.Silu),
                            sem="sC", waits=wt))
            zs_evac_s[cb] = sch.ctr["sC"]

        def pre_conv_xproj(j):
            cb = j % 2
            wexc = sch.now("sC")
            if j > 0:
                for blk in range(2):
                    xi_rd_v[1 - cb] = OP("vector", lambda e, blk=blk, cb=cb:
                        e.tensor_copy(a_xi[:, cb, blk, 0:3],
                                      a_xi[:, 1 - cb, blk, TCH:TCH + 3]),
                        sem="vC", waits=wexc)
            for blk in range(2):
                OP("scalar", lambda e, blk=blk, cb=cb:
                   e.activation(a_u[:, cb, blk, :], a_xi[:, cb, blk, 3:3 + TCH], AF.Identity,
                                scale=c_conv[:, blk, 3:4], bias=c_cb[:, blk:blk + 1]),
                   sem="sC", waits=[("vC", u_rd_v[cb])])
                for k in range(3):
                    tt = OP("vector", lambda e, blk=blk, k=k, cb=cb:
                       e.scalar_tensor_tensor(a_u[:, cb, blk, :], a_xi[:, cb, blk, k:k + TCH],
                                              c_conv[:, blk, k:k + 1], a_u[:, cb, blk, :],
                                              ALU.mult, ALU.add),
                       sem="vC", waits=sch.now("sC"))
                    xi_rd_v[cb] = max(xi_rd_v[cb], tt)
                silu_s[cb] = OP("scalar", lambda e, blk=blk, cb=cb:
                   e.activation(a_u[:, cb, blk, :], a_u[:, cb, blk, :], AF.Silu),
                   sem="sC", waits=sch.now("vC"))
            for f in range(NF):
                wf = [pm_free[4], ("gSSM", 16 * j), ("sC", silu_s[cb])]
                for k in range(2):
                    OP("tensor", lambda e, f=f, k=k, cb=cb:
                       e.matmul(pm[4][0:33, :], c_xpT[:, k, :],
                                a_u[:, cb, k, 512 * f:512 * (f + 1)],
                                start=(k == 0), stop=(k == 1)),
                       sem="tC", waits=wf if k == 0 else ())
                pm_free[4] = ("sC", OP("scalar", lambda e, f=f:
                    e.copy(a_ssm33[:, 512 * f:512 * (f + 1)], pm[4][0:33, :]),
                    sem="sC", waits=sch.now("tC")))
            GD(ssm_in[j][:, :], a_ssm33[:], "gSSM",
               waits=[("sC", sch.ctr["sC"]), ("gSSM", 16 * j)])
            OP("gpsimd", lambda e, j=j:
               e.collective_compute("AllReduce", ALU.add, replica_groups=RG,
                                    ins=[ssm_in[j].opt()], outs=[ssm_out[j].opt()]),
               sem="ccAR", waits=[("gSSM", 16 * (j + 1)), ("ccAR", j)])

        def pre_delta(j):
            cb = j % 2
            SD(a_dtr[:], ssm_out[j][0:1, :].partition_broadcast(128), "sDTR",
               waits=[("ccAR", j + 1), ("sDTR", 16 * j), ("sC", dtr_rd_s[0])])
            wdtr = [("sDTR", 16 * (j + 1))]
            for blk in range(2):
                OP("scalar", lambda e, blk=blk, cb=cb:
                   e.activation(a_dl[:, cb, blk, :], a_dtr[:], AF.Abs,
                                scale=c_dtw[:, blk:blk + 1], bias=c_dtb[:, blk:blk + 1]),
                   sem="sC", waits=wdtr)
                OP("scalar", lambda e, blk=blk, cb=cb:
                   e.activation(a_dl[:, cb, blk, :], a_dl[:, cb, blk, :], AF.Exp, scale=-1.0),
                   sem="sC")
                OP("scalar", lambda e, blk=blk, cb=cb:
                   e.activation(a_dl[:, cb, blk, :], a_dl[:, cb, blk, :], AF.Ln, bias=1.0),
                   sem="sC")
                dtr_rd_s[0] = OP("scalar", lambda e, blk=blk, cb=cb:
                   e.activation(a_ya[:, cb, blk, :], a_dtr[:], AF.Relu,
                                scale=c_dtw[:, blk:blk + 1], bias=c_dtb[:, blk:blk + 1]),
                   sem="sC", waits=[("gYG", 16 * (TCH // 512) * max(0, j - 1)),
                                    ("gYG2", 16 * (TCH // 512) * max(0, j - 1)),
                                    ("vC", gate_v[cb])])

        def pre_omega(j):
            cb = j % 2
            for blk in range(2):
                da = OP("vector", lambda e, blk=blk, cb=cb:
                   e.tensor_add(a_dl[:, cb, blk, :], a_dl[:, cb, blk, :], a_ya[:, cb, blk, :]),
                   sem="vC", waits=sch.now("sC"))
                yi = OP("scalar", lambda e, blk=blk, cb=cb:
                   e.activation(a_ya[:, cb, blk, :], a_u[:, cb, blk, :], AF.Identity,
                                scale=c_D[:, blk:blk + 1]),
                   sem="sC", waits=[("vC", da)])
                OP("vector", lambda e, blk=blk, cb=cb:
                   e.tensor_mul(a_u[:, cb, blk, :], a_dl[:, cb, blk, :], a_u[:, cb, blk, :]),
                   sem="vC", waits=[("sC", yi)])

        def emit_dA(j, n):
            cb = j % 2
            for blk in range(2):
                dA_s[(j, n, blk)] = OP("scalar", lambda e, n=n, blk=blk, cb=cb:
                   e.activation(a_dA[:, n % 3, blk, :], a_dl[:, cb, blk, :], AF.Exp,
                                scale=c_A[:, blk, n:n + 1]),
                   sem="sC", waits=[("vC", dA_rd_v[n % 3][blk])])

        def bc_row(j, row, slot, dst, stage_i, free_v):
            return SD(dst[:, slot, :],
                      ssm_out[j][row:row + 1, :].partition_broadcast(128),
                      f"s{'B' if stage_i == 0 else 'C'}{slot}",
                      waits=[("ccAR", j + 1), ("vC", free_v)])

        def emit_scan(j):
            cb = j % 2
            for n in range(3):
                emit_dA(j, n)
            thrB[0] = bc_row(j, 1, 0, a_B, 0, bfree_v[0])
            thrC[0] = bc_row(j, 1 + NST, 0, a_C, 1, cfree_v[0])
            for n in range(NST):
                nb = n % 2
                if n + 1 < NST:
                    thrB[1 - nb] = bc_row(j, 2 + n, 1 - nb, a_B, 0, bfree_v[1 - nb])
                    thrC[1 - nb] = bc_row(j, 2 + NST + n, 1 - nb, a_C, 1, cfree_v[1 - nb])
                for blk in range(2):
                    bm = OP("vector", lambda e, nb=nb, blk=blk, cb=cb:
                       e.tensor_mul(a_b[:, blk, :], a_u[:, cb, blk, :], a_B[:, nb, :]),
                       sem="vC", waits=([(f"sB{nb}", thrB[nb])] if blk == 0 else ()))
                    bfree_v[nb] = bm
                    if n == NST - 1:
                        u_rd_v[cb] = bm
                    sc = OP("vector", lambda e, n=n, blk=blk:
                       e.tensor_tensor_scan(a_b[:, blk, :], a_dA[:, n % 3, blk, :],
                                            a_b[:, blk, :], m_hst[:, blk, n:n + 1],
                                            ALU.mult, ALU.add),
                       sem="vC", waits=[("sC", dA_s[(j, n, blk)])])
                    dA_rd_v[n % 3][blk] = sc
                    OP("vector", lambda e, n=n, blk=blk:
                       e.tensor_copy(m_hst[:, blk, n:n + 1], a_b[:, blk, TCH - 1:TCH]),
                       sem="vC")
                    cfree_v[nb] = OP("vector", lambda e, nb=nb, blk=blk:
                       e.tensor_mul(a_b[:, blk, :], a_b[:, blk, :], a_C[:, nb, :]),
                       sem="vC", waits=([(f"sC{nb}", thrC[nb])] if blk == 0 else ()))
                    OP("vector", lambda e, blk=blk, cb=cb:
                       e.tensor_add(a_ya[:, cb, blk, :], a_ya[:, cb, blk, :], a_b[:, blk, :]),
                       sem="vC")
                if n + 3 < NST:
                    emit_dA(j, n + 3)
                if j + 1 < NCH:
                    if n == 0:
                        pre_inproj(j + 1, 0)
                    elif n == 3 and NF > 1:
                        pre_inproj(j + 1, 1)
                    elif n == 7:
                        pre_conv_xproj(j + 1)
                    elif n == 10:
                        pre_delta(j + 1)
                    elif n == 13:
                        pre_omega(j + 1)
            for blk in range(2):
                gate_v[cb] = OP("vector", lambda e, blk=blk, cb=cb:
                   e.tensor_mul(a_ya[:, cb, blk, :], a_ya[:, cb, blk, :], a_zs[:, cb, blk, :]),
                   sem="vC", waits=[("sC", zs_evac_s[cb])])
            cpc = TOK // TCH
            g0 = ((j % cpc) * TCH) // 512
            npc = TCH // 512
            ygsem = ("gYG", "gYG2")
            for gg in range(npc):
                for kh in range(DSL // 128):
                    GD(yg_pair[kh][j // cpc, g0 + gg, :, :],
                       a_ya[:, cb, kh, 512 * gg:512 * (gg + 1)], ygsem[kh],
                       waits=([("vC", gate_v[cb]), (ygsem[kh], 16 * npc * j)]
                              if gg == 0 else []))

        for _f in range(NF):
            pre_inproj(0, _f)
        pre_conv_xproj(0)
        pre_delta(0)
        pre_omega(0)
        NSPL = H // 128 + 4 * E
        wsub = []
        for sl in range(H // 128):
            wsub.append((w1s[sl], w1bf[sl].rearrange("p (h c) -> p h c", h=2), 0))
        for sl in range(E):
            for q in range(4):
                wsub.append((w2s[sl][:, 1024 * q:1024 * (q + 1)],
                             w2bf[sl].rearrange("p (h c) -> p h c", h=2), q))
        sv_in = AR[:, 42112:43136]
        sv_hi = AR[:, 43136:43648].bitcast(BF16)
        sv_lo = AR[:, 43648:44160].bitcast(BF16)

        def emit_split(i):
            src, dst, q = wsub[i]
            t_in = GD(sv_in[:, :], src, "sWS", waits=[("sWS", 48 * i)])
            hi_t = OP("gpsimd", lambda e: e.tensor_copy(sv_hi[:, :], sv_in[:, :]),
                      sem="gC", waits=[("sWS", t_in)])
            lo_t = OP("gpsimd", lambda e: e.tensor_sub(sv_lo[:, :], sv_in[:, :], sv_hi[:, :]),
                      sem="gC", waits=[("gC", hi_t)])
            GD(dst[:, 0, 1024 * q:1024 * (q + 1)], sv_hi[:, :], "sWS",
               waits=[("gC", lo_t)])
            GD(dst[:, 1, 1024 * q:1024 * (q + 1)], sv_lo[:, :], "sWS")

        for j in range(NCH):
            emit_scan(j)
            for i in range(j * NSPL // NCH, (j + 1) * NSPL // NCH):
                emit_split(i)

        for kh in range(DSL // 128):
            def _a2a_yg(e, kh=kh):
                with allow_ici_all_to_all():
                    return e.collective_compute("AllToAll", ALU.bypass, replica_groups=RG,
                                                ins=[yg_pair[kh].opt()],
                                                outs=[yg_me[kh].opt()])
            OP("gpsimd", _a2a_yg, sem="ccAG",
               waits=[(("gYG", "gYG2")[kh], 16 * (TCH // 512) * NCH)])

        wnc = [(n, v) for n, v in sch.all_now() if not n.startswith("cc")]
        for eng in ("sync", "tensor", "vector", "scalar", "gpsimd"):
            OP(eng, lambda e: None, waits=wnc)
        SD(b_wout[:], woutT.rearrange("(k p) c -> p k c", p=128), "sWO")
        SD(b_lex16[:], lex16, "sWO")
        wwo = sch.now("sWO")

        def ln_tile(src, dst, extra_waits=()):
            OP("vector", lambda e, s=src: e.reduce_sum(l_s1[:], s, AX.X),
               sem="vC", waits=list(extra_waits) + sch.now("sC"))
            OP("scalar", lambda e, s=src:
               e.activation(l_scr[:], s, AF.Square, accum_out=l_q[:]),
               sem="sC", waits=list(extra_waits) + sch.now("vC"))
            OP("vector", lambda e: e.tensor_scalar_mul(l_m[:], l_s1[:], 1.0 / D), sem="vC")
            OP("vector", lambda e: e.tensor_scalar_mul(l_v[:], l_q[:], 1.0 / D),
               sem="vC", waits=sch.now("sC"))
            OP("vector", lambda e: e.tensor_mul(l_t[:], l_m[:], l_m[:]), sem="vC")
            OP("vector", lambda e: e.tensor_sub(l_v[:], l_v[:], l_t[:]), sem="vC")
            OP("vector", lambda e: e.tensor_scalar_add(l_v[:], l_v[:], EPS), sem="vC")
            OP("scalar", lambda e: e.sqrt(l_v[:], l_v[:]), sem="sC", waits=sch.now("vC"))
            OP("vector", lambda e: e.reciprocal(l_r[:], l_v[:]), sem="vC",
               waits=sch.now("sC"))
            OP("vector", lambda e: e.tensor_mul(l_nb[:], l_m[:], l_r[:]), sem="vC")
            OP("vector", lambda e: e.tensor_scalar_mul(l_nb[:], l_nb[:], -1.0), sem="vC")
            OP("scalar", lambda e, s=src, d=dst:
               e.activation(d, s, AF.Identity, scale=l_r[:], bias=l_nb[:]),
               sem="sC", waits=sch.now("vC"))

        def router_tiles(grp):
            jts = range(4 * grp, 4 * (grp + 1))
            wxf = sch.now("sC")
            for jt in jts:
                for dk in range(8):
                    OP("tensor", lambda e, jt=jt, dk=dk:
                       e.transpose(pm[5][:, 0:128], b_xf[:, jt, 128 * dk:128 * (dk + 1)],
                                   c_id[:]),
                       sem="tC", waits=(wxf if (jt == jts[0] and dk == 0) else []) + [pm_free[5]])
                    pm_free[5] = ("sC", OP("scalar", lambda e, dk=dk:
                        e.copy(b_xfT[:, dk, :], pm[5][:, 0:128]),
                        sem="sC", waits=sch.now("tC")))
                wtr = sch.now("sC") + [pm_free[6]]
                for dk in range(8):
                    OP("tensor", lambda e, jt=jt, dk=dk:
                       e.matmul(pm[6][:, 0:E], b_xfT[:, dk, :], c_wg[:, dk, :],
                                start=(dk == 0), stop=(dk == 7)),
                       sem="tC", waits=wtr if dk == 0 else ())
                pm_free[6] = ("sC", OP("scalar", lambda e, jt=jt:
                    e.copy(b_lg[:, jt, :], pm[6][:, 0:E]),
                    sem="sC", waits=sch.now("tC")))
            wlg = sch.now("sC")
            for jt in jts:
                OP("vector", lambda e, jt=jt: e.reduce_max(r_v1[:, jt:jt + 1], b_lg[:, jt, :], AX.X),
                   sem="vC", waits=wlg if jt == jts[0] else ())
            for jt in jts:
                OP("vector", lambda e, jt=jt:
                   e.tensor_scalar_mul(r_v2[:, jt:jt + 1], r_v1[:, jt:jt + 1], -1.0), sem="vC")
            wneg = sch.now("vC")
            for jt in jts:
                OP("scalar", lambda e, jt=jt:
                   e.activation(b_pb[:, jt, :], b_lg[:, jt, :], AF.Exp,
                                bias=r_v2[:, jt:jt + 1], accum_out=r_p1[:, jt:jt + 1]),
                   sem="sC", waits=wneg if jt == jts[0] else ())
            wexp = sch.now("sC")
            OP("vector", lambda e, grp=grp:
               e.reciprocal(r_p2[:, 4 * grp:4 * (grp + 1)], r_p1[:, 4 * grp:4 * (grp + 1)]),
               sem="vC", waits=wexp)
            wrec = sch.now("vC")
            for jt in jts:
                OP("scalar", lambda e, jt=jt:
                   e.activation(b_pb[:, jt, :], b_pb[:, jt, :], AF.Identity,
                                scale=r_p2[:, jt:jt + 1]),
                   sem="sC", waits=wrec if jt == jts[0] else ())
            wpr = sch.now("sC")
            for jt in jts:
                OP("vector", lambda e, jt=jt:
                   e.reduce_max(r_v1[:, jt:jt + 1], b_pb[:, jt, :], AX.X),
                   sem="vC", waits=wpr if jt == jts[0] else ())
            for jt in jts:
                OP("vector", lambda e, jt=jt:
                   e.tensor_scalar(r_sel1[:, E * jt:E * (jt + 1)], b_pb[:, jt, :],
                                   r_v1[:, jt:jt + 1], None, ALU.is_equal),
                   sem="vC")
            for jt in jts:
                OP("vector", lambda e, jt=jt:
                   e.tensor_scalar(b_lg[:, jt, :], r_sel1[:, E * jt:E * (jt + 1)],
                                   -9.0, 9.0, ALU.mult, ALU.add),
                   sem="vC")
            for jt in jts:
                OP("vector", lambda e, jt=jt:
                   e.tensor_add(b_lg[:, jt, :], b_lg[:, jt, :], c_iota8[:]), sem="vC")
            for jt in jts:
                OP("vector", lambda e, jt=jt:
                   e.tensor_reduce(r_e1[:, jt:jt + 1], b_lg[:, jt, :], AX.X, ALU.min),
                   sem="vC")
            for jt in jts:
                OP("vector", lambda e, jt=jt:
                   e.tensor_scalar(r_sel1[:, E * jt:E * (jt + 1)], c_iota8[:],
                                   r_e1[:, jt:jt + 1], None, ALU.is_equal),
                   sem="vC")
            for jt in jts:
                OP("vector", lambda e, jt=jt:
                   e.scalar_tensor_tensor(b_pb[:, jt, :], r_sel1[:, E * jt:E * (jt + 1)],
                                          -2.0, b_pb[:, jt, :], ALU.mult, ALU.add),
                   sem="vC")
            for jt in jts:
                OP("vector", lambda e, jt=jt:
                   e.reduce_max(r_v2[:, jt:jt + 1], b_pb[:, jt, :], AX.X), sem="vC")
            for jt in jts:
                OP("vector", lambda e, jt=jt:
                   e.tensor_scalar(r_sel2[:, E * jt:E * (jt + 1)], b_pb[:, jt, :],
                                   r_v2[:, jt:jt + 1], None, ALU.is_equal), sem="vC")
            for jt in jts:
                OP("vector", lambda e, jt=jt:
                   e.tensor_scalar(b_lg[:, jt, :], r_sel2[:, E * jt:E * (jt + 1)],
                                   -9.0, 9.0, ALU.mult, ALU.add), sem="vC")
            for jt in jts:
                OP("vector", lambda e, jt=jt:
                   e.tensor_add(b_lg[:, jt, :], b_lg[:, jt, :], c_iota8[:]), sem="vC")
            for jt in jts:
                OP("vector", lambda e, jt=jt:
                   e.tensor_reduce(r_e2[:, jt:jt + 1], b_lg[:, jt, :], AX.X, ALU.min),
                   sem="vC")
            for jt in jts:
                OP("vector", lambda e, jt=jt:
                   e.tensor_scalar(r_sel2[:, E * jt:E * (jt + 1)], c_iota8[:],
                                   r_e2[:, jt:jt + 1], None, ALU.is_equal), sem="vC")


        ygk_free = [0, 0, 0, 0]
        evac_v = {-2: 0, -1: 0}
        rt_evac = {-1: 0}
        for grp in range(NGRP):
            SD(b_xr[:], xrows[512 * grp:512 * (grp + 1), :].rearrange(
                "(k p) c -> p k c", p=128), "sXR",
               waits=[("vC", evac_v[grp - 1]), ("sXR", 16 * grp)])
            korder = [2 * i for i in range(8)] + [2 * i + 1 for i in range(8)]
            for ki, k in enumerate(korder):
                buf = ki % 4
                kh = k % (DSL // 128)
                gthr = SD(b_ygk[:, buf, :],
                          yg_me[kh][k // (DSL // 128), grp, :, :],
                          f"gYGK{buf}",
                          waits=[("tC", ygk_free[buf]), ("ccAG", kh + 1)])
                wk = [(f"gYGK{buf}", gthr)]
                if ki == 0:
                    wk = wk + [("vC", evac_v[grp - 1]), ("sC", rt_evac[grp - 1])] + (
                        wwo if grp == 0 else [])
                for ts4 in range(4):
                    for fh in range(2):
                        OP("tensor", lambda e, k=k, buf=buf, ts4=ts4, fh=fh, ki=ki:
                           e.matmul(pm[ts4 * 2 + fh][:, :],
                                    b_ygk[:, buf, 128 * ts4:128 * (ts4 + 1)],
                                    b_wout[:, k, 512 * fh:512 * (fh + 1)],
                                    start=(ki == 0), stop=(ki == 15)),
                           sem="tC", waits=wk if (ts4 == 0 and fh == 0) else ())
                ygk_free[buf] = sch.ctr["tC"]
            wmm = sch.now("tC") + [("sXR", 16 * (grp + 1))]
            for ts4 in range(4):
                jt = grp * 4 + ts4
                for fh in range(2):
                    OP("vector", lambda e, jt=jt, ts4=ts4, fh=fh:
                       e.tensor_add(b_xf[:, jt, 512 * fh:512 * (fh + 1)],
                                    pm[ts4 * 2 + fh][:, :],
                                    b_xr[:, ts4, 512 * fh:512 * (fh + 1)]),
                       sem="vC", waits=wmm)
            evac_v[grp] = sch.ctr["vC"]
            for ts4 in range(4):
                jt = grp * 4 + ts4
                ln_tile(b_xf[:, jt, :], b_xf[:, jt, :])
                SD(xf_in[512 * grp + 128 * ts4:512 * grp + 128 * (ts4 + 1), :],
                   b_xf[:, jt, :], "sXFIN", waits=sch.now("sC"))
            OP("gpsimd", lambda e, grp=grp:
               e.collective_compute("AllGather", ALU.bypass, replica_groups=RG,
                                    ins=[xf_in[512 * grp:512 * (grp + 1), :].opt()],
                                    outs=[xf_out[512 * NCORE * grp:512 * NCORE * (grp + 1), :].opt()]),
               sem="ccXF", waits=[("sXFIN", 64 * (grp + 1))])
            router_tiles(grp)
            rt_evac[grp] = sch.ctr["sC"]

        OP("vector", lambda e: e.tensor_add(r_g1[:], r_v1[:], r_v2[:]), sem="vC")
        OP("vector", lambda e: e.reciprocal(r_g2[:], r_g1[:]), sem="vC")
        OP("vector", lambda e: e.tensor_mul(r_g1[:], r_v1[:], r_g2[:]), sem="vC")
        OP("vector", lambda e: e.tensor_mul(r_g2[:], r_v2[:], r_g2[:]), sem="vC")
        OP("vector", lambda e: e.tensor_add(r_ohb[:], r_sel1[:], r_sel2[:]), sem="vC")
        wohb = sch.now("vC")
        for jt in range(NTT):
            OP("tensor", lambda e, jt=jt:
               e.matmul(pm[7][0:1, 0:E], c_ones128[:], r_ohb[:, E * jt:E * (jt + 1)],
                        start=True, stop=True),
               sem="tC", waits=(wohb if jt == 0 else []) + [pm_free[7]])
            pm_free[7] = ("sC", OP("scalar", lambda e, jt=jt:
                e.copy(r_stg[0:1, E * jt:E * (jt + 1)], pm[7][0:1, 0:E]),
                sem="sC", waits=sch.now("tC")))
        SD(stg_d.rearrange("j e -> (j e)").unsqueeze(0), r_stg[0:1, :], "sTOT",
           waits=sch.now("sC"))
        SD(r_tots[:], stg_d[:, :], "sTOT", waits=[("sTOT", 16)])
        wtots = [("sTOT", 32)]
        OP("tensor", lambda e:
           e.matmul(pm[7][0:1, 0:E], c_ones16[:], r_tots[:], start=True, stop=True),
           sem="tC", waits=wtots + [pm_free[7]])
        pm_free[7] = ("sC", OP("scalar", lambda e: e.copy(r_c8[:], pm[7][0:1, 0:E]),
                               sem="sC", waits=sch.now("tC")))
        GD(cnt_in[:, :], r_c8[:], "gCNT", waits=[pm_free[7]])
        OP("gpsimd", lambda e:
           e.collective_compute("AllGather", ALU.bypass, replica_groups=RG,
                                ins=[cnt_in.opt()], outs=[cnt_out.opt()]),
           sem="ccCNT", waits=[("gCNT", 16)])
        SD(r_cnt[:], cnt_out[:, :], "sCNT", waits=[("ccCNT", 1)])
        wcnt = [("sCNT", 16)]
        for jt in range(NTT):
            bank = 5 + (jt % 2)
            wp = (wcnt if jt < 2 else []) + [pm_free[bank]]
            OP("tensor", lambda e, jt=jt, bank=bank:
               e.matmul(pm[bank][:, 0:E], c_lex[:], r_ohb[:, E * jt:E * (jt + 1)],
                        start=True, stop=False),
               sem="tC", waits=wp)
            OP("tensor", lambda e, jt=jt, bank=bank:
               e.matmul(pm[bank][:, 0:E], b_lex16[:, 128 * jt:128 * (jt + 1)],
                        r_tots[:], start=False, stop=False),
               sem="tC")
            OP("tensor", lambda e, bank=bank:
               e.matmul(pm[bank][:, 0:E], c_selrep[:], r_cnt[:],
                        start=False, stop=True),
               sem="tC")
            wmm2 = sch.now("tC")
            OP("vector", lambda e, jt=jt, bank=bank:
               e.tensor_mul(b_lg[:, jt, :], r_sel1[:, E * jt:E * (jt + 1)],
                            pm[bank][:, 0:E]),
               sem="vC", waits=wmm2)
            OP("vector", lambda e, jt=jt:
               e.reduce_sum(r_p1[:, jt:jt + 1], b_lg[:, jt, :], AX.X), sem="vC")
            OP("vector", lambda e, jt=jt, bank=bank:
               e.tensor_mul(b_lg[:, jt, :], r_sel2[:, E * jt:E * (jt + 1)],
                            pm[bank][:, 0:E]),
               sem="vC")
            pm_free[bank] = ("vC", OP("vector", lambda e, jt=jt:
                e.reduce_sum(r_p2[:, jt:jt + 1], b_lg[:, jt, :], AX.X), sem="vC"))
        OP("vector", lambda e: e.tensor_scalar(r_k1[:], r_p1[:], float(CAP), None, ALU.is_lt),
           sem="vC")
        OP("vector", lambda e: e.tensor_scalar(r_k2[:], r_p2[:], float(CAP), None, ALU.is_lt),
           sem="vC")
        OP("vector", lambda e: e.tensor_mul(r_gk1[:], r_g1[:], r_k1[:]), sem="vC")
        OP("vector", lambda e: e.tensor_mul(r_gk2[:], r_g2[:], r_k2[:]), sem="vC")
        for (rp, re_, rrow) in ((r_p1, r_e1, r_row1), (r_p2, r_e2, r_row2)):
            OP("vector", lambda e, rp=rp: e.tensor_scalar(l_scr[:, 0:NTT], rp[:],
               float(CAPP - 1), None, ALU.min), sem="vC")
            OP("vector", lambda e: e.tensor_scalar(l_scr[:, NTT:2 * NTT],
               l_scr[:, 0:NTT], 512.0, None, ALU.is_ge), sem="vC")
            for kk in range(2, RB // 4 + 1):
                OP("vector", lambda e, kk=kk: e.tensor_scalar(l_scr[:, 2 * NTT:3 * NTT],
                   l_scr[:, 0:NTT], float(512 * kk), None, ALU.is_ge), sem="vC")
                OP("vector", lambda e: e.tensor_add(l_scr[:, NTT:2 * NTT],
                   l_scr[:, NTT:2 * NTT], l_scr[:, 2 * NTT:3 * NTT]), sem="vC")
            OP("vector", lambda e: e.tensor_scalar(l_scr[:, 2 * NTT:3 * NTT],
               l_scr[:, NTT:2 * NTT], float(RB // 4), None, ALU.is_equal), sem="vC")
            OP("vector", lambda e, re_=re_:
               e.scalar_tensor_tensor(l_scr[:, 3 * NTT:4 * NTT], l_scr[:, NTT:2 * NTT],
                                      3584.0, l_scr[:, 0:NTT], ALU.mult, ALU.add), sem="vC")
            OP("vector", lambda e, re_=re_:
               e.scalar_tensor_tensor(l_scr[:, 3 * NTT:4 * NTT], re_[:], 512.0,
                                      l_scr[:, 3 * NTT:4 * NTT], ALU.mult, ALU.add), sem="vC")
            OP("vector", lambda e: e.tensor_scalar(l_scr[:, NTT:2 * NTT],
               l_scr[:, 2 * NTT:3 * NTT], -1.0, 1.0, ALU.mult, ALU.add), sem="vC")
            OP("vector", lambda e: e.tensor_mul(l_scr[:, 3 * NTT:4 * NTT],
               l_scr[:, 3 * NTT:4 * NTT], l_scr[:, NTT:2 * NTT]), sem="vC")
            LR = 128 * (RB - 4 * (RB // 4))
            C2 = 512 * (RB // 4) * (E - 1)
            OP("vector", lambda e, re_=re_, LR=LR:
               e.scalar_tensor_tensor(l_scr[:, NTT:2 * NTT], re_[:], float(LR),
                                      l_scr[:, 0:NTT], ALU.mult, ALU.add), sem="vC")
            OP("vector", lambda e, C2=C2: e.tensor_scalar_add(l_scr[:, NTT:2 * NTT],
               l_scr[:, NTT:2 * NTT], float(C2)), sem="vC")
            OP("vector", lambda e: e.tensor_mul(l_scr[:, NTT:2 * NTT],
               l_scr[:, NTT:2 * NTT], l_scr[:, 2 * NTT:3 * NTT]), sem="vC")
            OP("vector", lambda e: e.tensor_add(l_scr[:, 3 * NTT:4 * NTT],
               l_scr[:, 3 * NTT:4 * NTT], l_scr[:, NTT:2 * NTT]), sem="vC")
            OP("vector", lambda e, rrow=rrow:
               e.tensor_copy(rrow[:], l_scr[:, 3 * NTT:4 * NTT]), sem="vC")
        for (rp, re_, rk, dmp, rd) in ((r_p1, r_e1, r_k1, c_dump1, r_d1),
                                       (r_p2, r_e2, r_k2, c_dump2, r_d2)):
            OP("vector", lambda e, rp=rp, rk=rk:
               e.tensor_mul(l_scr[:, 0:NTT], rp[:], rk[:]), sem="vC")
            OP("vector", lambda e, rk=rk:
               e.tensor_scalar(l_scr[:, NTT:2 * NTT], rk[:], -1.0, 1.0, ALU.mult, ALU.add),
               sem="vC")
            OP("vector", lambda e, dmp=dmp:
               e.tensor_mul(l_scr[:, NTT:2 * NTT], l_scr[:, NTT:2 * NTT], dmp[:]), sem="vC")
            OP("vector", lambda e:
               e.tensor_add(l_scr[:, 0:NTT], l_scr[:, 0:NTT], l_scr[:, NTT:2 * NTT]),
               sem="vC")
            OP("vector", lambda e, re_=re_:
               e.scalar_tensor_tensor(l_scr[:, 0:NTT], re_[:], float(SHARD),
                                      l_scr[:, 0:NTT], ALU.mult, ALU.add), sem="vC")
            OP("vector", lambda e, rd=rd: e.tensor_copy(rd[:], l_scr[:, 0:NTT]), sem="vC")
        OP("vector", lambda e: e.memset(b_zero[:], 0.0), sem="vC")
        tbl_flat = tbl_in.rearrange("e w -> (e w)")
        GD(tbl_flat.rearrange("(p col) -> p col", p=128), b_zero[:], "gZERO",
           waits=sch.now("vC"))
        wz = [("gZERO", 16)] + sch.now("vC")
        for jt in range(NTT):
            OP("gpsimd", lambda e, jt=jt:
               e.indirect_dma_start(tbl_flat.unsqueeze(1),
                                    IndirectOffsetOnAxis(ap=r_d1[:, jt:jt + 1], axis=0),
                                    c_tokv[:, jt:jt + 1], None),
               sem="gSCAT", k=16, waits=wz if jt == 0 else ())
            OP("gpsimd", lambda e, jt=jt:
               e.indirect_dma_start(tbl_flat.unsqueeze(1),
                                    IndirectOffsetOnAxis(ap=r_d2[:, jt:jt + 1], axis=0),
                                    c_tokv[:, jt:jt + 1], None),
               sem="gSCAT", k=16)

        def _a2a(e):
            with allow_ici_all_to_all():
                return e.collective_compute("AllToAll", ALU.bypass, replica_groups=RG,
                                            ins=[tbl_in.opt()], outs=[tbl_out.opt()])
        OP("gpsimd", _a2a, sem="ccTBL", waits=[("gSCAT", 16 * 2 * NTT)])

        wbar = sch.all_now()
        for _eng in ("sync", "tensor", "vector", "scalar", "gpsimd"):
            OP(_eng, lambda e: None, waits=wbar)
        tb66 = lambda c: tbl_out[c].rearrange("(col p) -> col p", p=128)[0:TCOL, :]
        SD(e_tbs.rearrange('p c -> p c')[0:TCOL, :], tb66(0), "sTB", waits=[("ccTBL", 1)])
        nt = 1
        for c in range(1, NCORE):
            SD(e_tbt.rearrange('p c -> p c')[0:TCOL, :], tb66(c), "sTB",
               waits=sch.now("vC") + [("sTB", 16 * nt)])
            nt += 1
            OP("vector", lambda e: e.tensor_add(e_tbs.rearrange('p c -> p c')[0:TCOL, :], e_tbs.rearrange('p c -> p c')[0:TCOL, :],
                                                e_tbt.rearrange('p c -> p c')[0:TCOL, :]),
               sem="vC", waits=[("sTB", 16 * nt)])
        OP("vector", lambda e: e.tensor_scalar_add(e_tbs.rearrange('p c -> p c')[0:TCOL, :], e_tbs.rearrange('p c -> p c')[0:TCOL, :], -1.0),
           sem="vC")
        OP("vector", lambda e: e.tensor_scalar_max(e_tbs.rearrange('p c -> p c')[0:TCOL, :], e_tbs.rearrange('p c -> p c')[0:TCOL, :], 0.0),
           sem="vC")
        OP("tensor", lambda e:
           e.transpose(pm[6][:, 0:TCOL], e_tbs.rearrange('p c -> p c')[0:TCOL, :], c_id[0:TCOL, 0:TCOL]),
           sem="tC", waits=sch.now("vC") + [pm_free[6]])
        OP("scalar", lambda e: e.copy(e_tb128[:], pm[6][:, 0:TCOL]),
           sem="sC", waits=sch.now("tC"))
        pm_free[6] = ("sC", sch.ctr["sC"])
        OP("vector", lambda e: e.tensor_copy(e_tix[:], e_tb128[:]),
           sem="vC", waits=sch.now("sC"))
        wtix = sch.now("vC")

        w1_free = [0, 0]
        w2_free = [0, 0]
        hsp_free = [0, 0]
        rowg_thr = 0
        NRC = -(-RB // 4)
        ytail = [None]
        for rc in range(NRC):
            nb = min(4, RB - 4 * rc)
            rows = nb * 128
            for b in range(nb):
                rowg_thr = OP("gpsimd", lambda e, rc=rc, b=b:
                    e.indirect_dma_start(e_rows[:, b, :], None, xf_out[:, :],
                                         IndirectOffsetOnAxis(
                                             ap=e_tix[:, 4 * rc + b:4 * rc + b + 1], axis=0)),
                    sem="gROW", k=16,
                    waits=wtix + [("ccXF", NGRP)] + (sch.now("tC") if rc > 0 else []))
            if ytail[0] is not None:
                ytail[0]()
                ytail[0] = None
            wgth = [("gROW", rowg_thr)]
            for b in range(nb):
                for dk in range(8):
                    bank = 6 + ((b * 8 + dk) % 2)
                    OP("tensor", lambda e, b=b, dk=dk, bank=bank:
                       e.transpose(pm[bank][:, 0:128], e_rows[:, b, 128 * dk:128 * (dk + 1)],
                                   c_id[:]),
                       sem="tC", waits=wgth + [pm_free[bank]])
                    hh = OP("scalar", lambda e, b=b, dk=dk, bank=bank:
                        e.copy(e_xgh[:, dk, 128 * b:128 * (b + 1)], pm[bank][:, 0:128]),
                        sem="sC", waits=sch.now("tC"))
                    pm_free[bank] = ("vC", OP("vector", lambda e, b=b, dk=dk, bank=bank:
                        e.tensor_sub(e_xgl[:, dk, 128 * b:128 * (b + 1)],
                                     pm[bank][:, 0:128],
                                     e_xgh[:, dk, 128 * b:128 * (b + 1)]),
                        sem="vC", waits=[("sC", hh)]))
            wxgt = sch.now("vC", "sC")
            for hc in range(32):
                buf = hc % 2
                w1t = SD(e_w1[:, buf, :], w1bf[hc], f"sW1{buf}",
                         waits=[("tC", w1_free[buf]), ("sWS", 48 * NSPL)])
                wkk = [(f"sW1{buf}", w1t)] + (wxgt if hc < 2 else []) + [pm_free[hc % 2]]
                first = True
                for kk in range(8):
                    for (wof, xg) in ((0, e_xgh), (0, e_xgl), (1024, e_xgh)):
                        OP("tensor", lambda e, hc=hc, buf=buf, kk=kk, rows=rows,
                                     wof=wof, xg=xg, first=first:
                           e.matmul(pm[hc % 2][:, 0:rows],
                                    e_w1[:, buf, wof + 128 * kk:wof + 128 * (kk + 1)],
                                    xg[:, kk, 0:rows],
                                    start=first, stop=(kk == 7 and wof == 1024)),
                           sem="tC", waits=wkk if first else ())
                        first = False
                w1_free[buf] = sch.ctr["tC"]
                g1t = OP("scalar", lambda e, hc=hc, rows=rows:
                    e.activation(e_htmp[:, 0:rows], pm[hc % 2][:, 0:rows],
                                 AF.Gelu_apprx_tanh, bias=c_b1[:, hc:hc + 1]),
                    sem="sC", waits=sch.now("tC") + [("vC", hsp_free[hc % 2])])
                pm_free[hc % 2] = ("sC", g1t)
                g2t = OP("scalar", lambda e, hc=hc, rows=rows:
                    e.copy(e_hTh[:, hc, 0:rows], e_htmp[:, 0:rows]),
                    sem="sC")
                hsp_free[hc % 2] = OP("vector", lambda e, hc=hc, rows=rows:
                    e.tensor_sub(e_hTl[:, hc, 0:rows], e_htmp[:, 0:rows],
                                 e_hTh[:, hc, 0:rows]),
                    sem="vC", waits=[("sC", g2t)])
            whT = sch.now("sC", "vC")
            for dc in range(8):
                buf = dc % 2
                w2t = SD(e_w2[:, buf, :], w2bf[dc], f"sW2{buf}",
                         waits=[("tC", w2_free[buf]), ("sWS", 48 * NSPL)])
                wkk = [(f"sW2{buf}", w2t)] + (whT if dc < 2 else []) + [pm_free[2 + dc % 2]]
                first = True
                for hk in range(32):
                    for (wof, ht) in ((0, e_hTh), (0, e_hTl), (4096, e_hTh)):
                        OP("tensor", lambda e, dc=dc, buf=buf, hk=hk, rows=rows,
                                     wof=wof, ht=ht, first=first:
                           e.matmul(pm[2 + dc % 2][:, 0:rows],
                                    e_w2[:, buf, wof + 128 * hk:wof + 128 * (hk + 1)],
                                    ht[:, hk, 0:rows],
                                    start=first, stop=(hk == 31 and wof == 4096)),
                           sem="tC", waits=wkk if first else ())
                        first = False
                w2_free[buf] = sch.ctr["tC"]
                pm_free[2 + dc % 2] = ("sC", OP("scalar", lambda e, dc=dc, rows=rows:
                    e.activation(e_yT[:, dc, 0:rows], pm[2 + dc % 2][:, 0:rows],
                                 AF.Identity, bias=c_b2[:, dc:dc + 1]),
                    sem="sC", waits=sch.now("tC")))
            wyT = sch.now("sC")
            for dc in range(8):
                for b in range(nb):
                    bank = 4 + ((dc * nb + b) % 2)
                    OP("tensor", lambda e, dc=dc, b=b, bank=bank:
                       e.transpose(pm[bank][:, 0:128], e_yT[:, dc, 128 * b:128 * (b + 1)],
                                   c_id[:]),
                       sem="tC", waits=wyT + [pm_free[bank]] +
                       ([("gYIN", 64 * rc)] if dc == 0 and b == 0 else []))
                    pm_free[bank] = ("sC", OP("scalar", lambda e, dc=dc, b=b, bank=bank:
                        e.copy(e_yr[:, b, 128 * dc:128 * (dc + 1)], pm[bank][:, 0:128]),
                        sem="sC", waits=sch.now("tC")))
            ysc = sch.now("sC")

            def emit_ytail(rc=rc, rows=rows, nb=nb, ysc=ysc):
                for b in range(nb):
                    GD(y_in[512 * rc + 128 * b:512 * rc + 128 * (b + 1), :],
                       e_yr[:, b, :], "gYIN",
                       waits=(ysc + [("gYIN", 64 * rc)]) if b == 0 else ())
                off = E * 512 * rc
                OP("gpsimd", lambda e, rc=rc, rows=rows, off=off:
                   e.collective_compute("AllGather", ALU.bypass, replica_groups=RG,
                                        ins=[y_in[512 * rc:512 * rc + rows, :].opt()],
                                        outs=[y_out[off:off + E * rows, :].opt()]),
                   sem="ccY", waits=[("gYIN", 64 * rc + 16 * nb)])
            ytail[0] = emit_ytail
        ytail[0]()

        barrier()
        gthr = {}; fxthr = {}
        cons_v = {-4: 0, -3: 0, -2: 0, -1: 0}; cons_s = {-4: 0, -3: 0, -2: 0, -1: 0}
        fxsem = ("sXF4", "sXF5", "sXF6", "sXF7")

        def issue_g4(jt):
            b = jt % 4
            OP("gpsimd", lambda e, jt=jt, b=b:
                e.indirect_dma_start(f_r1[:, b, :], None, y_out[:, :],
                                     IndirectOffsetOnAxis(ap=r_row1[:, jt:jt + 1], axis=0)),
                sem=f"gR4{b}", k=16,
                waits=[("ccY", NRC), ("sC", cons_s[jt - 4]), ("vC", cons_v[jt - 4])])
            gthr[jt] = OP("gpsimd", lambda e, jt=jt, b=b:
                e.indirect_dma_start(f_r2[:, b, :], None, y_out[:, :],
                                     IndirectOffsetOnAxis(ap=r_row2[:, jt:jt + 1], axis=0)),
                sem=f"gR4{b}", k=16)
            fxthr[jt] = SD(f_xf[:, b, :], xf_in[128 * jt:128 * (jt + 1), :], fxsem[b],
                           waits=[("vC", cons_v[jt - 4])])

        for j0 in range(min(4, NTT)):
            issue_g4(j0)
        for jt in range(NTT):
            b = jt % 4
            bc = jt % 2
            cons_s[jt] = OP("scalar", lambda e, jt=jt, b=b, bc=bc:
               e.activation(f_cmb[:, bc, :], f_r1[:, b, :], AF.Identity,
                            scale=r_gk1[:, jt:jt + 1]),
               sem="sC", waits=[(f"gR4{b}", gthr[jt]), ("sOUT", 16 * jt)])
            OP("vector", lambda e, jt=jt, b=b, bc=bc:
               e.scalar_tensor_tensor(f_cmb[:, bc, :], f_r2[:, b, :], r_gk2[:, jt:jt + 1],
                                      f_cmb[:, bc, :], ALU.mult, ALU.add),
               sem="vC", waits=sch.now("sC"))
            OP("vector", lambda e, jt=jt, b=b, bc=bc:
               e.tensor_add(f_cmb[:, bc, :], f_cmb[:, bc, :], f_xf[:, b, :]), sem="vC",
               waits=[(fxsem[b], fxthr[jt])])
            ln_tile(f_cmb[:, bc, :], f_cmb[:, bc, :])
            cons_v[jt] = sch.ctr["vC"]
            cons_s[jt] = sch.ctr["sC"]
            if jt + 4 < NTT:
                issue_g4(jt + 4)
            SD(out[128 * jt:128 * (jt + 1), :], f_cmb[:, bc, :], "sOUT",
               waits=sch.now("sC"))
        OP("sync", lambda e: None, waits=[("sOUT", 16 * NTT)])
        OP("gpsimd", lambda e: None, waits=[("sOUT", 16 * NTT)])

        with nc.Block() as block:
            @block.sync
            def _(e):
                for eng, fn in sch.items:
                    if eng == "sync":
                        fn(e)

            @block.tensor
            def _(e):
                for eng, fn in sch.items:
                    if eng == "tensor":
                        fn(e)

            @block.vector
            def _(e):
                for eng, fn in sch.items:
                    if eng == "vector":
                        fn(e)

            @block.scalar
            def _(e):
                for eng, fn in sch.items:
                    if eng == "scalar":
                        fn(e)

            @block.gpsimd
            def _(e):
                for eng, fn in sch.items:
                    if eng == "gpsimd":
                        fn(e)

    return nc


_NC_CACHE = {}

def _get_nc():
    if "nc" not in _NC_CACHE:
        _NC_CACHE["nc"] = build()
    return _NC_CACHE["nc"]


def kernel(L0_attn_x, L0_attn_in_proj_w, L0_attn_conv_w, L0_attn_conv_b,
           L0_attn_x_proj_w, L0_attn_dt_w, L0_attn_dt_b, L0_attn_A_log,
           L0_attn_D_param, L0_attn_out_proj_w, L0_ffn_Wg, L0_ffn_W1, L0_ffn_b1,
           L0_ffn_W2, L0_ffn_b2):
    x = np.ascontiguousarray(np.asarray(L0_attn_x, dtype=np.float32).reshape(T, D))
    xTh = np.ascontiguousarray(x.T)
    Win = np.asarray(L0_attn_in_proj_w, dtype=np.float32)
    convw_h = np.ascontiguousarray(np.asarray(L0_attn_conv_w, dtype=np.float32)[:, 0, :])
    convb_h = np.asarray(L0_attn_conv_b, dtype=np.float32).reshape(DI, 1)
    xpw = np.asarray(L0_attn_x_proj_w, dtype=np.float32)
    dtw_h = np.asarray(L0_attn_dt_w, dtype=np.float32).reshape(DI, 1)
    dtb_h = np.asarray(L0_attn_dt_b, dtype=np.float32).reshape(DI, 1)
    alog_h = np.asarray(L0_attn_A_log, dtype=np.float32)
    dpar_h = np.asarray(L0_attn_D_param, dtype=np.float32).reshape(DI, 1)
    woutT_h = np.ascontiguousarray(np.asarray(L0_attn_out_proj_w, dtype=np.float32).T)
    wg_h = np.asarray(L0_ffn_Wg, dtype=np.float32)
    W1 = np.asarray(L0_ffn_W1, dtype=np.float32)
    b1 = np.asarray(L0_ffn_b1, dtype=np.float32)
    W2 = np.asarray(L0_ffn_W2, dtype=np.float32)
    b2 = np.asarray(L0_ffn_b2, dtype=np.float32)

    lane = np.arange(128)
    ident_h = np.eye(128, dtype=np.float32)
    lexcl_h = (lane[:, None] < lane[None, :]).astype(np.float32)
    lex16_h = np.zeros((NTT, NTT * 128), np.float32)
    for jj in range(NTT):
        lex16_h[:jj, 128 * jj:128 * (jj + 1)] = 1.0
    ones16_h = np.ones((NTT, 1), np.float32)
    ones128_h = np.ones((128, 1), np.float32)
    iota8_h = np.broadcast_to(np.arange(E, dtype=np.float32), (128, E)).copy()
    dump1_h = np.empty((128, NTT), np.float32)
    dump2_h = np.empty((128, NTT), np.float32)
    for jj in range(NTT):
        dump1_h[:, jj] = CAPP + 2 * (128 * jj + lane)
        dump2_h[:, jj] = CAPP + 2 * (128 * jj + lane) + 1

    in_maps = []
    for c in range(NCORE):
        dsl = slice(DSL * c, DSL * (c + 1))
        zsl = slice(DI + DSL * c, DI + DSL * (c + 1))
        winT_c = np.ascontiguousarray(np.concatenate([Win[dsl], Win[zsl]], axis=0).T)
        tokv_h = np.empty((128, NTT), np.float32)
        for jj in range(NTT):
            gg = jj // 4
            q = 128 * jj + lane - 512 * gg
            tokv_h[:, jj] = 512 * NCORE * gg + 512 * c + q + 1

        selrep_h = np.zeros((E, 128), np.float32)
        selrep_h[:c, :] = 1.0
        w1s_h = np.ascontiguousarray(
            W1[c].reshape(8, 128, 32, 128).transpose(2, 1, 0, 3).reshape(32, 128, 1024))
        w2s_h = np.ascontiguousarray(
            W2[c].reshape(32, 128, 8, 128).transpose(2, 1, 0, 3).reshape(8, 128, 4096))
        b1r_h = np.ascontiguousarray(b1[c].reshape(32, 128).T)
        b2r_h = np.ascontiguousarray(b2[c].reshape(8, 128).T)
        in_maps.append(dict(
            xT=xTh, xrows=np.ascontiguousarray(x[TOK * c:TOK * (c + 1)]),
            winT=winT_c,
            convw=np.ascontiguousarray(convw_h[dsl].reshape(2, 128, DC).transpose(1, 0, 2)),
            convb=np.ascontiguousarray(convb_h[dsl].reshape(2, 128).transpose(1, 0)),
            xpT=np.ascontiguousarray(xpw[:, dsl].T.reshape(2, 128, 33).transpose(1, 0, 2)),
            dtw=np.ascontiguousarray(dtw_h[dsl].reshape(2, 128).transpose(1, 0)),
            dtb=np.ascontiguousarray(dtb_h[dsl].reshape(2, 128).transpose(1, 0)),
            alog=np.ascontiguousarray(alog_h[dsl].reshape(2, 128, NST).transpose(1, 0, 2)),
            dpar=np.ascontiguousarray(dpar_h[dsl].reshape(2, 128).transpose(1, 0)),
            woutT=woutT_h, wg=wg_h, w1s=w1s_h, w2s=w2s_h, b1r=b1r_h, b2r=b2r_h,
            ident=ident_h, lexcl=lexcl_h, lex16=lex16_h, selrep=selrep_h,
            ones16=ones16_h, ones128=ones128_h, iota8=iota8_h,
            dump1=dump1_h, dump2=dump2_h, tokv=tokv_h,
        ))

    nc_prog = _get_nc()
    res = run_bass_kernel_spmd(nc_prog, in_maps, core_ids=list(range(NCORE)))
    outp = np.concatenate([np.asarray(res.results[c]["out"]) for c in range(NCORE)], axis=0)
    return outp.reshape(1, T, D).astype(np.float32)
